# version 19
# baseline (speedup 1.0000x reference)
import numpy as np
import concourse.bass as bass
import concourse.mybir as mybir
from concourse.bass_utils import run_bass_kernel_spmd

F32 = mybir.dt.float32
BF16 = mybir.dt.bfloat16
I32 = mybir.dt.int32
AF = mybir.ActivationFunctionType
ALU = mybir.AluOpType
AX = mybir.AxisListType

D = 1024
HID = 2048
QK = 128
NE = 16
NF = 1024
ND = 8
EPS = 1e-5
TIE = 1e-6
BIG = 1e9
KB = 1024


class Prog:
    ENGINES = ("tensor", "vector", "scalar", "gpsimd", "sync")

    def __init__(self, nc):
        self.nc = nc
        self.stream = {e: [] for e in self.ENGINES}
        self.sems = {}
        self.count = {}
        self.waited = {}

    def sem(self, name):
        if name not in self.sems:
            self.sems[name] = self.nc.alloc_semaphore(name)
            self.count[name] = 0
        return self.sems[name]

    CHAIN = {"vector": "VE", "scalar": "SE", "gpsimd": "GE"}

    def emit(self, eng, fn, waits=(), inc=None, inc_n=1, chain=True):
        ws = []
        if chain and eng in self.CHAIN:
            own = self.CHAIN[eng]
            assert inc is None or inc == own
            inc = own
            inc_n = 1
            self.sem(own)
            if self.count[own] > 0:
                ws.append((own, self.count[own]))
        for t in waits:
            if t is None:
                continue
            s, v = t
            key = (eng, s)
            if self.waited.get(key, 0) >= v:
                continue
            self.waited[key] = v
            ws.append((s, v))
        tok = None
        if inc is not None:
            self.sem(inc)
            self.count[inc] += inc_n
            tok = (inc, self.count[inc])
        self.stream[eng].append((ws, fn, tok, inc_n))
        return tok

    def replay(self, eng):
        def body(obj):
            for ws, fn, tok, inc_n in self.stream[eng]:
                for (s, v) in ws:
                    obj.wait_ge(self.sems[s], v)
                inst = fn(obj)
                if tok is not None:
                    inst.then_inc(self.sems[tok[0]], inc_n)
        return body


def build_program(T):
    NT = T // 128
    NSL = T // 512
    NTH = T // 1024
    RN = 1.0 / T
    nc = bass.Bass("TRN2", target_bir_lowering=False)

    def din(name, shape):
        return nc.dram_tensor(name, shape, F32, kind="ExternalInput")

    x_h = din("x", [T, D])
    lng_h = din("ln_g", [D]); lnb_h = din("ln_b", [D])
    wh_h = din("Wh", [D, 2 * HID]); bh_h = din("bh", [2 * HID])
    wqk_h = din("Wqk", [D, QK]); bqk_h = din("bqk", [QK])
    gam_h = din("gamma", [2, QK]); bet_h = din("beta", [2, QK])
    wo_h = din("Wo", [HID, D]); bo_h = din("bo", [D])
    rw_h = din("router_w", [NE, D]); rb_h = din("bias_r", [NE])
    gw_h = din("gate_w", [NE, D, NF]); uw_h = din("up_w", [NE, D, NF])
    dw_h = din("down_w", [NE, NF, D])
    shg_h = din("sh_gate", [D, 2 * NF]); shu_h = din("sh_up", [D, 2 * NF])
    shd_h = din("sh_down", [2 * NF, D])
    out_h = nc.dram_tensor("out", [T, D], F32, kind="ExternalOutput")

    sp_xn = nc.dram_tensor("sp_xn", [T, D], BF16, kind="Internal")
    sp_lo = nc.dram_tensor("sp_lo", [T, D], BF16, kind="Internal")
    sp_nm = nc.dram_tensor("sp_nm", [T, D], BF16, kind="Internal")
    sp_gate = nc.dram_tensor("sp_gate", [HID, T], BF16, kind="Internal")
    sp_s = nc.dram_tensor("sp_s", [T, D], BF16, kind="Internal")

    def sb(name, shape, dtype, off):
        assert off >= 16512 and off % 32 == 0, (name, off)
        return nc.alloc_sbuf_tensor_at(name, shape, dtype, offset=off)

    B0 = 16512
    RA = B0 + 16896
    RB = RA + 32 * KB
    RC = RB + 32 * KB
    RD = RC + 64 * KB
    assert RD + 60 * KB <= 229344

    ones_bf = sb("ones_bf", [1, 128], BF16, B0 + 0)
    ones_f = sb("ones_f", [1, 128], F32, B0 + 256)
    svecT = sb("svecT", [128, 8], F32, B0 + 768)
    svecN = sb("svecN", [128, 4], F32, B0 + 800)
    bhg = sb("bhg", [128, 16], F32, B0 + 832)
    epsc = sb("epsc", [128, 1], F32, B0 + 896)
    nhalf = sb("nhalf", [128, 1], F32, B0 + 928)
    phalf = sb("phalf", [128, 1], F32, B0 + 960)
    bias16 = sb("bias16", [128, 16], F32, B0 + 992)
    c_sb = sb("c_sb", [128, NT * 16], F32, B0 + 1056)
    masks = sb("masks", [128, 4 * 512], BF16, B0 + 2080)
    qT = sb("qT", [128, T], BF16, B0 + 2080 + 4 * KB)
    kT = sb("kT", [128, T], BF16, B0 + 2080 + 8 * KB)
    bo_row = sb("bo_row", [1, D], BF16, B0 + 2080 + 12 * KB)
    ident_bf = sb("ident_bf", [128, 128], BF16, B0 + 16 * KB + 32)


    wo_sb = sb("wo_sb", [128, 16 * D], BF16, RA)
    xnT_pf = sb("xnT_pf", [128, ND * T], BF16, RA)

    normedT = sb("normedT", [128, ND * T], BF16, RB)
    relu_t = sb("relu_t", [128, 1024], F32, RB)
    xnin = [sb(f"xnin{i}", [128, D], BF16, RB + 4 * KB + i * 2 * KB) for i in range(8)]
    sout = [sb(f"sout{i}", [128, D], BF16, RB + 20 * KB + i * 2 * KB) for i in range(2)]
    acc = sb("acc", [128, NT * D], BF16, RB)

    xnT_lo = sb("xnT_lo", [128, ND * T], BF16, RC)
    _xwo = [RC + 32 * KB, RC + 36 * KB, RA, RA + 4 * KB]
    xw = [sb(f"xw{i}", [128, D], F32, _xwo[i]) for i in range(4)]
    _xno = [RC + 40 * KB, RC + 44 * KB, RA + 8 * KB, RA + 12 * KB]
    xnf = [sb(f"xnf{i}", [128, D], F32, _xno[i]) for i in range(4)]
    sqj = sb("sqj", [128, D], F32, RC + 48 * KB)
    _hio = [RC + 52 * KB, RC + 54 * KB, RA + 16 * KB, RA + 18 * KB]
    hi_t = [sb(f"hi_t{i}", [128, D], BF16, _hio[i]) for i in range(4)]
    _loo = [RC + 56 * KB, RC + 58 * KB, RA + 20 * KB, RA + 22 * KB]
    lo_t = [sb(f"lo_t{i}", [128, D], BF16, _loo[i]) for i in range(4)]
    _nmo = [RC + 60 * KB, RC + 62 * KB, RA + 24 * KB, RA + 26 * KB]
    nm_t = [sb(f"nm_t{i}", [128, D], BF16, _nmo[i]) for i in range(4)]
    v_sb = sb("v_sb", [128, NT * HID], BF16, RC)
    za = sb("za", [128, 16 * T], BF16, RC)
    z_sb = sb("z_sb", [128, ND * T], BF16, RC)
    de = sb("de", [128, ND * D], BF16, RC + 32 * KB)
    sin = [sb(f"sin{i}", [128, D], BF16, RC + 48 * KB + i * 2 * KB) for i in range(2)]
    xin2 = [sb(f"xin2{i}", [128, D], F32, RC + 52 * KB + i * 4 * KB) for i in range(2)]
    gfin = sb("gfin", [128, D], BF16, RC + 60 * KB)

    rstage = sb("rstage", [16, D], F32, RD + 16 * KB)
    identf = sb("identf", [128, 128], F32, RD + 20 * KB)
    svec_stage = sb("svec_stage", [8, 128], F32, RD + 20 * KB + 512)
    bhg_stage = sb("bhg_stage", [16, 128], F32, RD + 21 * KB)
    lgb_stage = sb("lgb_stage", [16, 128], F32, RD + 21 * KB + 512)
    lgb = sb("lgb", [128, 16], F32, B0 + 16 * KB + 288)
    rT_f32 = sb("rT_f32", [128, 128], F32, RD + 60 * KB)
    rT_hi = sb("rT_hi", [128, 128], BF16, RD + 60 * KB + 512)
    rT_lo = sb("rT_lo", [128, 128], BF16, RD + 60 * KB + 768)
    brow16 = sb("brow16", [1, 16], F32, RD + 61 * KB)
    iota_i = sb("iota_i", [128, 512], I32, RD + 32 * KB)
    iota_f = sb("iota_f", [128, 512], F32, RD + 34 * KB)
    eio_i = sb("eio_i", [128, NT * 16], I32, RD + 36 * KB)
    eio_f = sb("eio_f", [128, NT * 16], F32, RD + 62 * KB)
    gio_i = sb("gio_i", [128, NT * 4], I32, RD + 38 * KB)
    gio_f = sb("gio_f", [128, NT * 4], F32, RD + 63 * KB)
    stb = RD + 38 * KB + 512
    st_ssx = sb("st_ssx", [128, NT], F32, stb)
    st_ssq = sb("st_ssq", [128, NT], F32, stb + 64)
    st_m = sb("st_m", [128, NT], F32, stb + 128)
    st_msq = sb("st_msq", [128, NT], F32, stb + 192)
    st_var = sb("st_var", [128, NT], F32, stb + 256)
    st_l = sb("st_l", [128, NT], F32, stb + 320)
    st_r1 = sb("st_r1", [128, NT], F32, stb + 384)
    st_sq1 = sb("st_sq1", [128, NT], F32, stb + 448)
    st_v2 = sb("st_v2", [128, NT], F32, stb + 512)
    st_Sg = sb("st_Sg", [128, NT], F32, stb + 576)
    st_rho = sb("st_rho", [128, NT], F32, stb + 640)
    logit_sb = sb("logit_sb", [128, NT * 16], F32, RB)
    r_sig = sb("r_sig", [128, NT * 16], F32, RB + 1 * KB)
    r_sv = sb("r_sv", [128, NT * 16], F32, RB + 2 * KB)
    r_t1 = sb("r_t1", [128, NT * 16], F32, RB + 3 * KB)
    r_t2 = sb("r_t2", [128, NT * 16], F32, RB + 4 * KB)
    r_msk = sb("r_msk", [128, NT * 16], F32, RB + 5 * KB)
    r_gs = sb("r_gs", [128, NT * 4], F32, RB + 6 * KB)
    r_gse = sb("r_gse", [128, NT * 4], F32, RB + 6 * KB + 256)
    r_gp = sb("r_gp", [128, NT * 4], F32, RB + 6 * KB + 512)
    r_m1 = sb("r_m1", [128, NT], F32, RB + 6 * KB + 768)
    r_m2 = sb("r_m2", [128, NT], F32, RB + 6 * KB + 832)
    wv = [sb(f"wv{i}", [128, ND * 1024], BF16, RD + i * 16 * KB) for i in range(2)]
    attnT = sb("attnT", [128, 16 * 512], BF16, RD)
    gbuf = [sb(f"gbuf{i}", [128, 16 * 512], BF16, RD + 16 * KB + i * 16 * KB)
            for i in range(2)]
    whg_a = sb("whg_a", [128, ND * 1024], BF16, RD)
    whg_b = sb("whg_b", [128, ND * 1024], BF16, RD + 16 * KB)
    gout = [sb(f"gout{i}", [128, 1024], BF16, RD + 52 * KB + i * 2 * KB)
            for i in range(2)]
    qk_sil = sb("qk_sil", [128, 512], F32, RD + 58 * KB)
    bh_row = sb("bh_row", [1, HID], BF16, RD + 48 * KB)
    wqk_sb = sb("wqk_sb", [128, ND * 128], BF16, RD + 56 * KB)
    shd_sb = sb("shd_sb", [128, 16 * D], BF16, RD + 16 * KB)
    ge = [sb(f"ge{i}", [128, ND * NF], BF16, RD + i * 16 * KB) for i in range(2)]
    ue_t = sb("ue_t", [128, ND * NF], BF16, RD + 32 * KB)
    shgc = [sb(f"shgc{i}", [128, ND * 128], BF16, RD + 48 * KB + i * 2 * KB)
            for i in range(2)]
    shuc = [sb(f"shuc{i}", [128, ND * 128], BF16, RD + 52 * KB + i * 2 * KB)
            for i in range(2)]
    mfin = sb("mfin", [128, D], F32, RD + 56 * KB)
    obuf = [sb(f"obuf{i}", [128, D], F32, RD + 32 * KB + i * 4 * KB)
            for i in range(2)]

    PS = [nc.alloc_psum_tensor(f"PS{i}", [128, 1024], F32) for i in range(4)]

    P = Prog(nc)
    for s in ("TE", "VE", "SE", "GE"):
        P.sem(s)

    A = lambda t: t.ap()

    def cast_dma(sem, dst_ap, src_ap, waits=()):
        return P.emit("gpsimd", lambda g, d=dst_ap, s=src_ap: g.dma_start(d, s),
                      waits=waits, inc=sem, inc_n=16, chain=False)

    def sdma(sem, dst_ap, src_ap, waits=()):
        return P.emit("sync", lambda s, d=dst_ap, so=src_ap: s.dma_start(d, so),
                      waits=waits, inc=sem, inc_n=16)

    def sdma_tp(sem, dst_ap, src_ap, waits=()):
        return P.emit("sync", lambda s, d=dst_ap, so=src_ap: s.dma_start(
            d, so, transpose=True), waits=waits, inc=sem, inc_n=16)

    def adma(sem, dst_ap, src_ap, waits=()):
        return P.emit("scalar", lambda s, d=dst_ap, so=src_ap: s.dma_start(d, so),
                      waits=waits, inc=sem, inc_n=16, chain=False)

    def tok_all(sem):
        return (sem, P.count.get(sem, 0))

    tok_wqk = cast_dma("WQK", A(wqk_sb), bass.AP(wqk_h, 0, [[QK, 128], [QK * 128, ND], [1, QK]]))
    tok_wv = [None, None]
    tok_wv[0] = cast_dma("WV0", A(wv[0]), bass.AP(wh_h, 0,
        [[2 * HID, 128], [2 * HID * 128, ND], [1, 1024]]))
    tok_bh = cast_dma("BH", A(bh_row), bass.AP(bh_h, 0, [[HID, 1], [1, HID]]))
    tok_bo = cast_dma("BO", A(bo_row), bass.AP(bo_h, 0, [[D, 1], [1, D]]))

    tok_ones = P.emit("vector", lambda v: v.memset(A(ones_bf), 1.0), inc="VE")
    P.emit("vector", lambda v: v.memset(A(ones_f), 1.0))
    P.emit("vector", lambda v: v.memset(A(epsc), EPS))
    P.emit("vector", lambda v: v.memset(A(nhalf), -0.5))
    tok_consts = P.emit("vector", lambda v: v.memset(A(phalf), 0.5), inc="VE")

    tok_io = P.emit("gpsimd", lambda g: g.iota(
        A(iota_i), [[1, 512]], channel_multiplier=-1,
        allow_small_or_imprecise_dtypes=True), inc="GE")
    tok_eio = P.emit("gpsimd", lambda g: g.iota(
        A(eio_i), [[0, NT], [1, 16]], channel_multiplier=0,
        allow_small_or_imprecise_dtypes=True), inc="GE")
    tok_gio = P.emit("gpsimd", lambda g: g.iota(
        A(gio_i), [[0, NT], [1, 4]], channel_multiplier=0,
        allow_small_or_imprecise_dtypes=True), inc="GE")
    P.emit("vector", lambda v: v.tensor_copy(A(iota_f), A(iota_i)), waits=[tok_io])
    tok_identf = P.emit("vector", lambda v: v.tensor_scalar(
        A(identf), A(iota_f)[:, 0:128], 0.0, None, ALU.is_equal), inc="VE")
    tok_identb = P.emit("vector", lambda v: v.tensor_scalar(
        A(ident_bf), A(iota_f)[:, 0:128], 0.0, None, ALU.is_equal), inc="VE")
    tok_masks = None
    for oi, o in enumerate((0, 128, 256, 384)):
        tok_masks = P.emit("vector", lambda v, oi=oi, o=o: v.tensor_scalar(
            A(masks)[:, oi * 512:(oi + 1) * 512], A(iota_f), float(o), None, ALU.is_ge),
            inc="VE")
    P.emit("vector", lambda v: v.tensor_copy(A(eio_f), A(eio_i)), waits=[tok_eio])
    tok_iof = P.emit("vector", lambda v: v.tensor_copy(A(gio_f), A(gio_i)),
                     waits=[tok_gio], inc="VE")

    tok_lng = sdma("LGBA", A(lgb_stage)[0:8, :], bass.AP(lng_h, 0, [[128, 8], [1, 128]]))
    tok_lnb = sdma("CONST", A(lgb_stage)[8:16, :], bass.AP(lnb_h, 0, [[128, 8], [1, 128]]),
                   waits=[tok_lng])
    tok_rst = sdma("CONST", A(rstage)[0:16, :], A(rw_h))
    tok_brow = sdma("CONST", A(brow16), bass.AP(rb_h, 0, [[NE, 1], [1, NE]]))
    tok_svs = None
    for row, src in ((0, bass.AP(bqk_h, 0, [[QK, 1], [1, QK]])),
                     (1, A(gam_h)[0:1, :]), (2, A(bet_h)[0:1, :]),
                     (3, A(gam_h)[1:2, :]), (4, A(bet_h)[1:2, :])):
        tok_svs = sdma("CONST", A(svec_stage)[row:row + 1, :], src)
    tok_bhgs = sdma("CONST", A(bhg_stage)[0:16, :], bass.AP(bh_h, HID, [[128, 16], [1, 128]]))

    tok_const = tok_all("CONST")
    tok_lng = tok_lnb = tok_rst = tok_brow = tok_svs = tok_bhgs = tok_const

    tok_bi16m = P.emit("tensor", lambda t: t.matmul(
        A(PS[1])[:, 0:16], A(ones_f), A(brow16), start=True, stop=True),
        waits=[tok_brow, tok_consts], inc="TE")
    tok_bi16 = P.emit("vector", lambda v: v.tensor_copy(A(bias16), A(PS[1])[:, 0:16]),
                      waits=[tok_bi16m], inc="VE")
    tok_lgbT = P.emit("tensor", lambda t: t.transpose(
        A(PS[1])[:, 48:64], A(lgb_stage)[0:16, :], A(identf)[0:16, 0:16]),
        waits=[tok_lng, tok_lnb, tok_identf, tok_bi16], inc="TE")
    tok_lgb = P.emit("vector", lambda v: v.tensor_copy(A(lgb), A(PS[1])[:, 48:64]),
                     waits=[tok_lgbT], inc="VE")
    tok_svT = P.emit("tensor", lambda t: t.transpose(
        A(PS[1])[:, 16:24], A(svec_stage), A(identf)[0:8, 0:8]),
        waits=[tok_svs, tok_identf, tok_bi16], inc="TE")
    tok_svec = P.emit("vector", lambda v: v.tensor_copy(A(svecT), A(PS[1])[:, 16:24]),
                      waits=[tok_svT], inc="VE")
    tok_svn = None
    for i in range(4):
        tok_svn = P.emit("vector", lambda v, i=i: v.tensor_scalar(
            A(svecN)[:, i:i + 1], A(svecT)[:, i + 1:i + 2], RN, None, ALU.mult),
            inc="VE")
    tok_bhT = P.emit("tensor", lambda t: t.transpose(
        A(PS[1])[:, 32:48], A(bhg_stage)[0:16, :], A(identf)[0:16, 0:16]),
        waits=[tok_bhgs, tok_svec], inc="TE")
    tok_bhg = P.emit("vector", lambda v: v.tensor_copy(A(bhg), A(PS[1])[:, 32:48]),
                     waits=[tok_bhT], inc="VE")

    tok_sq = [None] * NT
    tok_sx = [None] * NT
    tok_hi = [None] * NT
    tok_lo = [None] * NT
    tok_nm = [None] * NT
    tok_sphi = [None] * NT
    tok_splo = [None] * NT
    tok_spnm = [None] * NT
    for k in range(NT):
        sl = k % 4
        w = [tok_sq[k - 4], tok_sx[k - 4]] if k >= 4 else []
        tok_x = sdma(f"XA{sl}", A(xw[sl]), A(x_h)[k * 128:(k + 1) * 128, :], waits=w)
        tok_sq[k] = P.emit("scalar", lambda s, sl=sl, k=k: s.activation(
            A(sqj), A(xw[sl]), AF.Square, accum_out=A(st_ssq)[:, k:k + 1]),
            waits=[tok_x], inc="SE")
        tok_sx[k] = P.emit("vector", lambda v, sl=sl, k=k: v.reduce_sum(
            A(st_ssx)[:, k:k + 1], A(xw[sl]), AX.X), waits=[tok_x], inc="VE")
    P.emit("vector", lambda v: v.tensor_scalar(A(st_m), A(st_ssx), 1.0 / D, None,
                                               ALU.mult), waits=[tok_sx[NT - 1]])
    P.emit("vector", lambda v: v.tensor_scalar(A(st_msq), A(st_ssq), 1.0 / D, None,
                                               ALU.mult), waits=[tok_sq[NT - 1]])
    P.emit("vector", lambda v: v.tensor_tensor(A(st_var), A(st_m), A(st_m), ALU.mult))
    tok_var = P.emit("vector", lambda v: v.tensor_tensor(A(st_var), A(st_msq),
                                                         A(st_var), ALU.subtract),
                     inc="VE")
    P.emit("scalar", lambda s: s.activation(A(st_l), A(st_var), AF.Ln, bias=A(epsc)),
           waits=[tok_var])
    P.emit("scalar", lambda s: s.activation(A(st_r1), A(st_l), AF.Exp, scale=A(nhalf)))
    tok_r1 = P.emit("scalar", lambda s: s.activation(A(st_sq1), A(st_l), AF.Exp,
                                                     scale=A(phalf)), inc="SE")
    P.emit("vector", lambda v: v.tensor_tensor(A(st_v2), A(st_var), A(st_r1), ALU.mult),
           waits=[tok_r1])
    tok_v2 = P.emit("vector", lambda v: v.tensor_tensor(A(st_v2), A(st_v2), A(st_r1),
                                                        ALU.mult), inc="VE")
    P.emit("scalar", lambda s: s.activation(A(st_l), A(st_v2), AF.Ln, bias=A(epsc)),
           waits=[tok_v2])
    tok_Sg = P.emit("scalar", lambda s: s.activation(A(st_Sg), A(st_l), AF.Exp,
                                                     scale=A(nhalf)), inc="SE")
    tok_rho = P.emit("vector", lambda v: v.tensor_tensor(A(st_rho), A(st_Sg),
                                                         A(st_sq1), ALU.mult),
                     waits=[tok_Sg], inc="VE")
    for k in range(NT):
        sl = k % 4
        w = [tok_rho]
        if k < 4:
            w += [tok_sq[NT - 4 + k], tok_sx[NT - 4 + k]]
        else:
            w += [tok_nm[k - 4], tok_hi[k - 4], tok_sphi[k - 4], tok_splo[k - 4],
                  tok_spnm[k - 4]]
        tok_x = sdma(f"XA{sl}", A(xw[sl]), A(x_h)[k * 128:(k + 1) * 128, :], waits=w)
        tok_xn = P.emit("vector", lambda v, sl=sl, k=k: v.tensor_scalar(
            A(xnf[sl]), A(xw[sl]), A(st_m)[:, k:k + 1], A(st_r1)[:, k:k + 1],
            ALU.subtract, ALU.mult), waits=[tok_x], inc="VE")
        tok_nm[k] = P.emit("vector", lambda v, sl=sl, k=k: v.tensor_scalar(
            A(nm_t[sl]), A(xnf[sl]), A(st_rho)[:, k:k + 1], None, ALU.mult),
            inc="VE")
        tok_hi[k] = P.emit("scalar", lambda s, sl=sl: s.copy(A(hi_t[sl]), A(xnf[sl])),
                           waits=[tok_xn], inc="SE")
        tok_lo[k] = P.emit("gpsimd", lambda g, sl=sl: g.tensor_tensor(
            A(lo_t[sl]), A(xnf[sl]), A(hi_t[sl]), ALU.subtract),
            waits=[tok_hi[k]], inc="GE")
        hh = k // max(1, NT // 2)
        tok_spnm[k] = adma(f"SPN{sl}{hh}", A(sp_nm)[k * 128:(k + 1) * 128, :],
                           A(nm_t[sl]), waits=[tok_nm[k]])
        tok_sphi[k] = adma(f"SPH{sl}{hh}", A(sp_xn)[k * 128:(k + 1) * 128, :],
                           A(hi_t[sl]), waits=[tok_hi[k]])
        tok_splo[k] = adma(f"SPL{sl}{hh}", A(sp_lo)[k * 128:(k + 1) * 128, :],
                           A(lo_t[sl]), waits=[tok_lo[k]])
    sph_h0 = [tok_all(f"SPH{s}0") for s in range(4)]
    spl_h0 = [tok_all(f"SPL{s}0") for s in range(4)]
    spn_h0 = [tok_all(f"SPN{s}0") for s in range(4)]


    sph_all = sph_h0 + [tok_all(f"SPH{s}1") for s in range(4)]
    spl_all = spl_h0 + [tok_all(f"SPL{s}1") for s in range(4)]
    spn_all = spn_h0 + [tok_all(f"SPN{s}1") for s in range(4)]
    tok_wo = cast_dma("WO", A(wo_sb),
                      bass.AP(wo_h, 0, [[D, 128], [D * 128, 16], [1, D]]),
                      waits=sph_all + spl_all + spn_all)
    TH2 = T // 2
    for dc in range(ND):
        sdma_tp("TPN0", A(normedT)[:, dc * T: dc * T + TH2],
                A(sp_nm)[0:TH2, dc * 128:(dc + 1) * 128], waits=spn_h0)
    tok_tpn0 = tok_all("TPN0")
    tok_tp_nm_h0 = None
    for dc in range(ND):
        tok_tp_nm_h0 = P.emit("vector", lambda v, dc=dc: v.tensor_scalar(
            A(normedT)[:, dc * T: dc * T + TH2],
            A(normedT)[:, dc * T: dc * T + TH2],
            A(lgb)[:, dc:dc + 1], A(lgb)[:, 8 + dc: 9 + dc], ALU.mult, ALU.add),
            waits=[tok_tpn0, tok_lgb], inc="VE")
    for dc in range(ND):
        sdma_tp("TPN1", A(normedT)[:, dc * T + TH2: (dc + 1) * T],
                A(sp_nm)[TH2:T, dc * 128:(dc + 1) * 128], waits=spn_all)
    tok_tpn1 = tok_all("TPN1")
    tok_tp_nm_h1 = None
    for dc in range(ND):
        tok_tp_nm_h1 = P.emit("vector", lambda v, dc=dc: v.tensor_scalar(
            A(normedT)[:, dc * T + TH2: (dc + 1) * T],
            A(normedT)[:, dc * T + TH2: (dc + 1) * T],
            A(lgb)[:, dc:dc + 1], A(lgb)[:, 8 + dc: 9 + dc], ALU.mult, ALU.add),
            waits=[tok_tpn1], inc="VE")
    tok_tp_nm = [tok_tp_nm_h0, tok_tp_nm_h1]

    tok_rTt = None
    for dc in range(ND):
        tok_rTt = P.emit("tensor", lambda t, dc=dc: t.transpose(
            A(PS[1])[:, 64 + dc * 16: 64 + (dc + 1) * 16],
            A(rstage)[0:16, dc * 128:(dc + 1) * 128], A(identf)[0:16, 0:16]),
            waits=[tok_rst, tok_identf, tok_bhg], inc=("TE" if dc == ND - 1 else None))
    tok_wv[1] = cast_dma("WV1", A(wv[1]), bass.AP(wh_h, 1024,
        [[2 * HID, 128], [2 * HID * 128, ND], [1, 1024]]), waits=[tok_rTt])
    tok_rTe = P.emit("vector", lambda v: v.tensor_copy(A(rT_f32), A(PS[1])[:, 64:192]),
                     waits=[tok_rTt], inc="VE")
    P.emit("vector", lambda v: v.tensor_copy(A(rT_hi), A(rT_f32)))
    tok_rT = P.emit("vector", lambda v: v.tensor_tensor(A(rT_lo), A(rT_f32), A(rT_hi),
                                                        ALU.subtract), inc="VE")

    tok_logits = None

    def emit_logits(tts, w0):
        nonlocal tok_logits
        for tt in tts:
            mms = []
            for dc in range(ND):
                mms.append((A(xnT_pf)[:, dc * T + tt * 128: dc * T + (tt + 1) * 128],
                            A(rT_hi)[:, dc * 16:(dc + 1) * 16]))
            for dc in range(ND):
                mms.append((A(xnT_pf)[:, dc * T + tt * 128: dc * T + (tt + 1) * 128],
                            A(rT_lo)[:, dc * 16:(dc + 1) * 16]))
            for dc in range(ND):
                mms.append((A(xnT_lo)[:, dc * T + tt * 128: dc * T + (tt + 1) * 128],
                            A(rT_hi)[:, dc * 16:(dc + 1) * 16]))
            n = len(mms)
            for i, (l, r) in enumerate(mms):
                w = w0 if (tt == tts[0] and i == 0) else ()
                tok_logits = P.emit("tensor", lambda t, l=l, r=r, st=(i == 0),
                                    sp=(i == n - 1), tt=tt: t.matmul(
                    A(PS[2])[:, tt * 16:(tt + 1) * 16], l, r, start=st, stop=sp),
                    waits=w, inc=("TE" if i == n - 1 else None))

    pevac = {0: None, 1: tok_rTe, 2: None, 3: None}
    tok_qk_done = [None] * NSL

    def emit_qk(slcs, w0):
        for slc in slcs:
            ps = PS[slc % 2]
            w = [tok_wqk, pevac[slc % 2]] + list(w0)
            tg = None
            for dc in range(ND):
                tg = P.emit("tensor", lambda t, dc=dc, slc=slc, ps=ps: t.matmul(
                    A(ps)[:, 0:512], A(wqk_sb)[:, dc * 128:(dc + 1) * 128],
                    A(normedT)[:, dc * T + slc * 512: dc * T + (slc + 1) * 512],
                    start=(dc == 0), stop=(dc == ND - 1)),
                    waits=(w if dc == 0 else ()), inc=("TE" if dc == ND - 1 else None))
            tok_se = P.emit("scalar", lambda s, ps=ps: s.activation(
                A(qk_sil), A(ps)[:, 0:512], AF.Silu, bias=A(svecT)[:, 0:1]),
                waits=[tg, tok_svn, tok_svec,
                       (tok_qk_done[slc - 1] if slc >= 1 else None)], inc="SE")
            pevac[slc % 2] = tok_se
            P.emit("vector", lambda v, slc=slc: v.tensor_scalar(
                A(qT)[:, slc * 512:(slc + 1) * 512], A(qk_sil),
                A(svecN)[:, 0:1], A(svecN)[:, 1:2], ALU.mult, ALU.add), waits=[tok_se])
            tok_qk_done[slc] = P.emit("vector", lambda v, slc=slc: v.tensor_scalar(
                A(kT)[:, slc * 512:(slc + 1) * 512], A(qk_sil),
                A(svecN)[:, 2:3], A(svecN)[:, 3:4], ALU.mult, ALU.add), inc="VE")

    emit_qk(list(range(max(1, NSL // 2))),
            ([tok_tp_nm_h0] if NSL >= 2 else tok_tp_nm))
    emit_qk(list(range(max(1, NSL // 2), NSL)), tok_tp_nm)

    tok_qkT = tok_qk_done[NSL - 1]

    tok_v = None
    tok_hvv0 = None
    tok_hvv1 = None
    for hs in range(2):
        for tt in range(NT):
            pi = tt % 2
            ps = PS[pi]
            w = [tok_wv[hs], tok_bh, tok_ones, pevac[pi]] + tok_tp_nm
            if tt == min(8, NT - 1):
                w = w + sph_all + spl_all + spn_all
            tg = None
            for hh in range(2):
                for dc in range(ND):
                    tg = P.emit("tensor", lambda t, dc=dc, tt=tt, hh=hh, hs=hs, ps=ps:
                                t.matmul(
                        A(ps)[:, hh * 512:(hh + 1) * 512],
                        A(normedT)[:, dc * T + tt * 128: dc * T + (tt + 1) * 128],
                        A(wv[hs])[:, dc * 1024 + hh * 512: dc * 1024 + (hh + 1) * 512],
                        start=(dc == 0), stop=False),
                        waits=(w if (hh == 0 and dc == 0) else ()))
                tg = P.emit("tensor", lambda t, hh=hh, hs=hs, ps=ps: t.matmul(
                    A(ps)[:, hh * 512:(hh + 1) * 512], A(ones_bf),
                    A(bh_row)[:, hs * 1024 + hh * 512: hs * 1024 + (hh + 1) * 512],
                    start=False, stop=True), inc=("TE" if hh == 1 else None))
            if hs == 0 and tt == NT - 1:
                tok_hvv0 = tg
            if hs == 1 and tt == NT - 1:
                tok_hvv1 = tg
            tok_v = P.emit("scalar", lambda s, ps=ps, tt=tt, hs=hs: s.activation(
                A(v_sb)[:, tt * HID + hs * 1024: tt * HID + (hs + 1) * 1024],
                A(ps), AF.Silu), waits=[tg], inc="SE")
            pevac[pi] = tok_v

    tok_whga = cast_dma("WHGA", A(whg_a), bass.AP(wh_h, HID,
        [[2 * HID, 128], [2 * HID * 128, ND], [1, 1024]]), waits=[tok_hvv0])
    tok_whgb = cast_dma("WHGB", A(whg_b), bass.AP(wh_h, HID + 1024,
        [[2 * HID, 128], [2 * HID * 128, ND], [1, 1024]]), waits=[tok_hvv1])
    gout_cons = [None, None]
    tok_gsp = None
    gs_half = None
    gidx = 0
    for slh in range(max(1, NTH)):
        for ht in range(16):
            wbuf = whg_a if ht < 8 else whg_b
            wtok = tok_whga if ht < 8 else tok_whgb
            pi = gidx % 2
            ps = PS[pi]
            gsl = gidx % 2
            gidx += 1
            w = [wtok, pevac[pi]] + tok_tp_nm
            tg = None
            ng = min(2, (T - slh * 1024) // 512)
            for g in range(ng):
                for dc in range(ND):
                    tg = P.emit("tensor", lambda t, dc=dc, slh=slh, g=g, ps=ps,
                                wbuf=wbuf, ht=ht: t.matmul(
                        A(ps)[:, g * 512:(g + 1) * 512],
                        A(wbuf)[:, dc * 1024 + (ht % 8) * 128:
                                dc * 1024 + (ht % 8) * 128 + 128],
                        A(normedT)[:, dc * T + slh * 1024 + g * 512:
                                   dc * T + slh * 1024 + (g + 1) * 512],
                        start=(dc == 0), stop=(dc == ND - 1)),
                        waits=(w if (g == 0 and dc == 0) else ()),
                        inc=("TE" if (g == ng - 1 and dc == ND - 1) else None))
            go = gout[gsl]
            tok_gev = P.emit("scalar", lambda s, ps=ps, go=go, ht=ht, ng=ng: s.activation(
                A(go)[:, 0:ng * 512], A(ps)[:, 0:ng * 512], AF.Silu,
                bias=A(bhg)[:, ht:ht + 1]),
                waits=[tg, tok_bhg, gout_cons[gsl]], inc="SE")
            pevac[pi] = tok_gev
            tok_gsp = sdma(f"GS{gsl}{slh}", A(sp_gate)[ht * 128:(ht + 1) * 128,
                                      slh * 1024: slh * 1024 + ng * 512],
                           A(go)[:, 0:ng * 512], waits=[tok_gev])
            gout_cons[gsl] = tok_gsp
        if slh == 0:
            gs_half = [tok_all("GS00"), tok_all("GS10")]

    gs_all = gs_half + [tok_all("GS01"), tok_all("GS11")]
    tok_gin = [None, None]
    gin_cons = [None, None]
    strip_cons = None
    tok_s_spill = [None] * NT
    xnin_cons = [None] * 8
    sout_cons = [None, None]
    tok_gau_last = None
    tok_xnin = [None] * 8
    pending = None

    def emit_gau(p):
        nonlocal tok_gau_last, pending
        isl, slg, tok_gt = p
        for q in range(4):
            tt = isl * 4 + q
            xs = tt % 8
            ps = PS[3]
            tg = None
            for ds in range(2):
                for ht in range(16):
                    w = ([tok_gt, tok_wo, tok_bo, pevac[3]]
                         if (ds == 0 and ht == 0) else ())
                    tg = P.emit("tensor", lambda t, ht=ht, ds=ds, q=q, slg=slg, ps=ps:
                                t.matmul(
                        A(ps)[:, ds * 512:(ds + 1) * 512],
                        A(gbuf[slg])[:, ht * 512 + q * 128: ht * 512 + (q + 1) * 128],
                        A(wo_sb)[:, ht * 1024 + ds * 512: ht * 1024 + (ds + 1) * 512],
                        start=(ht == 0), stop=False), waits=w)
                tg = P.emit("tensor", lambda t, ds=ds, ps=ps: t.matmul(
                    A(ps)[:, ds * 512:(ds + 1) * 512], A(ones_bf),
                    A(bo_row)[:, ds * 512:(ds + 1) * 512], start=False, stop=True),
                    inc=("TE" if ds == 1 else None))
            tok_gau_last = tg
            so = sout[tt % 2]
            tok_sev = P.emit("vector", lambda v, ps=ps, so=so, xs=xs: v.tensor_tensor(
                A(so), A(ps), A(xnin[xs]), ALU.add),
                waits=[tg, tok_xnin[xs], sout_cons[tt % 2]], inc="VE")
            pevac[3] = tok_sev
            xnin_cons[xs] = tok_sev
            tok_s_spill[tt] = adma(f"SS{tt % 2}", A(sp_s)[tt * 128:(tt + 1) * 128, :],
                                   A(so), waits=[tok_sev])
            sout_cons[tt % 2] = tok_s_spill[tt]
        gin_cons[slg] = tok_gau_last

    for isl in range(NSL):
        i0 = isl * 512
        jmax = 4 * (isl + 1)
        slg = isl % 2
        tok_gin[slg] = sdma(f"GIN{slg}", A(gbuf[slg]),
                            bass.AP(sp_gate, i0, [[T, 128], [128 * T, 16], [1, 512]]),
                            waits=(gs_half if isl < 2 else gs_all)
                            + [gin_cons[slg]])
        for q in range(4):
            tt = isl * 4 + q
            xs = tt % 8
            tok_xnin[xs] = sdma(f"XN{xs}", A(xnin[xs]),
                                A(sp_xn)[tt * 128:(tt + 1) * 128, :],
                                waits=sph_all + [xnin_cons[xs]])
        tok_strip = None
        for jp in range(0, jmax, 2):
            pi = (jp // 2) % 2
            ps = PS[pi]
            npair = min(2, jmax - jp)
            tg = None
            for u in range(npair):
                jt = jp + u
                w = [tok_qkT, pevac[pi]]
                tg = P.emit("tensor", lambda t, jt=jt, u=u, ps=ps, isl=isl: t.matmul(
                    A(ps)[:, u * 512:(u + 1) * 512],
                    A(kT)[:, jt * 128:(jt + 1) * 128],
                    A(qT)[:, isl * 512:(isl + 1) * 512],
                    start=True, stop=True),
                    waits=(w if u == 0 else ()), inc=("TE" if u == npair - 1 else None))
            w = [tg, tok_masks, strip_cons] + (gs_all if (isl == 0 and jp == 0) else [])
            P.emit("vector", lambda v, ps=ps, npair=npair: v.tensor_scalar(
                A(relu_t)[:, 0:npair * 512], A(ps)[:, 0:npair * 512], 0.0, None,
                ALU.max), waits=w)
            tok_strip = P.emit("vector", lambda v, jp=jp, npair=npair: v.tensor_tensor(
                A(attnT)[:, jp * 512:(jp + npair) * 512], A(relu_t)[:, 0:npair * 512],
                A(relu_t)[:, 0:npair * 512], ALU.mult), inc="VE")
            pevac[pi] = tok_strip
            for u in range(npair):
                jt = jp + u
                oi = jt - 4 * isl
                if oi >= 0:
                    tok_strip = P.emit("vector", lambda v, jt=jt, oi=oi: v.tensor_tensor(
                        A(attnT)[:, jt * 512:(jt + 1) * 512],
                        A(attnT)[:, jt * 512:(jt + 1) * 512],
                        A(masks)[:, oi * 512:(oi + 1) * 512], ALU.mult), inc="VE")
                    pevac[pi] = tok_strip
        tok_gt_last = None
        for hp in range(0, 16, 2):
            ps = PS[2]
            tg = None
            for hh in range(2):
                ht = hp + hh
                for jt in range(jmax):
                    w = [tok_strip, tok_v, pevac[2]] if (jt == 0 and hh == 0) else ()
                    tg = P.emit("tensor", lambda t, jt=jt, ht=ht, hh=hh, ps=ps,
                                jmax=jmax: t.matmul(
                        A(ps)[:, hh * 512:(hh + 1) * 512],
                        A(v_sb)[:, jt * HID + ht * 128: jt * HID + (ht + 1) * 128],
                        A(attnT)[:, jt * 512:(jt + 1) * 512],
                        start=(jt == 0), stop=(jt == jmax - 1)),
                        waits=w, inc=("TE" if (hh == 1 and jt == jmax - 1) else None))
            strip_cons = tg
            tok_gt_last = P.emit("vector", lambda v, hp=hp, ps=ps, slg=slg:
                                 v.tensor_tensor(
                A(gbuf[slg])[:, hp * 512:(hp + 2) * 512], A(ps),
                A(gbuf[slg])[:, hp * 512:(hp + 2) * 512], ALU.mult),
                waits=[tg, tok_gin[slg]], inc="VE")
            pevac[2] = tok_gt_last
        if pending is not None:
            emit_gau(pending)
        pending = (isl, slg, tok_gt_last)
    emit_gau(pending)

    for dc in range(ND):
        sdma_tp("TPF", A(xnT_pf)[:, dc * T:(dc + 1) * T],
                A(sp_xn)[:, dc * 128:(dc + 1) * 128],
                waits=sph_all + [tok_gau_last])
    tok_xnpf = [tok_all("TPF")]
    for dc in range(ND):
        sdma_tp("TPL0", A(xnT_lo)[:, dc * T: dc * T + TH2],
                A(sp_lo)[0:TH2, dc * 128:(dc + 1) * 128],
                waits=spl_h0 + [strip_cons])
    for dc in range(ND):
        sdma_tp("TPL1", A(xnT_lo)[:, dc * T + TH2: (dc + 1) * T],
                A(sp_lo)[TH2:T, dc * 128:(dc + 1) * 128], waits=spl_all)
    tok_tp_lo = [tok_all("TPL0"), tok_all("TPL1")]

    pevac = {0: None, 1: None, 2: None, 3: None}
    tok_a_last = None
    shgc_cons = [None, None]
    tok_shgc = [None, None]
    shuc_pre = [None, None]
    FT_ORDER = list(range(8, 16)) + list(range(8)) if NT > 4 else list(range(16))
    for i2 in range(2):
        ft = FT_ORDER[i2]
        tok_shgc[i2] = cast_dma(f"SHGC{i2}", A(shgc[i2]),
            bass.AP(shg_h, ft * 128, [[2 * NF, 128], [2 * NF * 128, ND], [1, 128]]),
            waits=[tok_gau_last] + tok_xnpf + tok_tp_lo)
        shuc_pre[i2] = cast_dma(f"SHUC{i2}", A(shuc[i2]),
            bass.AP(shu_h, i2 * 128, [[2 * NF, 128], [2 * NF * 128, ND], [1, 128]]),
            waits=[tok_gau_last] + tok_xnpf + tok_tp_lo)
    tok_shd = cast_dma("SHD", A(shd_sb),
                       bass.AP(shd_h, 0, [[D, 128], [D * 128, 16], [1, D]]),
                       waits=[tok_gau_last] + tok_xnpf)
    npi = 0

    def emit_aT(fts, base):
        nonlocal tok_a_last, npi
        for idx, ft in enumerate(fts):
            slw = (base + idx) % 2
            if base + idx >= 2:
                tok_shgc[slw] = cast_dma(f"SHGC{slw}", A(shgc[slw]),
                    bass.AP(shg_h, ft * 128,
                            [[2 * NF, 128], [2 * NF * 128, ND], [1, 128]]),
                    waits=[shgc_cons[slw]])
            for th in range(max(1, NTH)):
                pi = npi % 2
                npi += 1
                ps = PS[pi]
                tg = None
                ng = min(2, (T - th * 1024) // 512)
                for g in range(ng):
                    for dc in range(ND):
                        w = (([tok_shgc[slw], pevac[pi]] + tok_xnpf)
                             if (g == 0 and dc == 0) else ())
                        tg = P.emit("tensor", lambda t, ft=ft, th=th, g=g, dc=dc, ps=ps,
                                    slw=slw: t.matmul(
                            A(ps)[:, g * 512:(g + 1) * 512],
                            A(shgc[slw])[:, dc * 128:(dc + 1) * 128],
                            A(xnT_pf)[:, dc * T + th * 1024 + g * 512:
                                      dc * T + th * 1024 + (g + 1) * 512],
                            start=(dc == 0), stop=(dc == ND - 1)), waits=w,
                            inc=("TE" if (g == ng - 1 and dc == ND - 1) else None))
                shgc_cons[slw] = tg
                tok_a_last = P.emit("scalar", lambda s, ps=ps, ft=ft, th=th, ng=ng:
                                    s.activation(
                    A(za)[:, ft * T + th * 1024: ft * T + th * 1024 + ng * 512],
                    A(ps)[:, 0:ng * 512], AF.Silu), waits=[tg], inc="SE")
                pevac[pi] = tok_a_last

    if NT > 4:
        emit_aT(FT_ORDER[:8], 0)
    emit_logits(list(range(NT)), tok_xnpf + tok_tp_lo + [tok_rT])
    tok_lg = P.emit("vector", lambda v: v.tensor_copy(A(logit_sb), A(PS[2])[:, 0:NT * 16]),
                    waits=[tok_logits], inc="VE")
    tok_sig = P.emit("scalar", lambda s: s.activation(A(r_sig), A(logit_sb), AF.Sigmoid),
                     waits=[tok_lg], inc="SE")

    def v3(t, inner):
        return A(t).rearrange("p (g c) -> p g c", c=inner)

    def bcast(t, ng, rep):
        return bass.AP(t, 0, [[ng, 128], [1, ng], [0, rep]])

    P.emit("vector", lambda v: v.tensor_tensor(
        v3(r_sv, 16), v3(r_sig, 16),
        bass.AP(bias16, 0, [[16, 128], [0, NT], [1, 16]]), ALU.add),
        waits=[tok_sig, tok_bi16, tok_iof])

    def cview(i):
        return bass.AP(r_sv, i, [[NT * 16, 128], [4, NT * 4]])

    P.emit("vector", lambda v: v.tensor_tensor(A(r_gs), cview(0), cview(1), ALU.add))
    for (a, b) in ((0, 2), (0, 3), (1, 2), (1, 3), (2, 3)):
        P.emit("vector", lambda v, a=a, b=b: v.tensor_tensor(A(r_gp), cview(a), cview(b),
                                                             ALU.add))
        P.emit("vector", lambda v: v.tensor_tensor(A(r_gs), A(r_gs), A(r_gp), ALU.max))
    P.emit("vector", lambda v: v.scalar_tensor_tensor(A(r_gse), A(gio_f), -TIE, A(r_gs),
                                                      ALU.mult, ALU.add))
    P.emit("vector", lambda v: v.reduce_max(A(r_m1), v3(r_gse, 4), AX.X))
    P.emit("vector", lambda v: v.tensor_tensor(v3(r_gp, 4), v3(r_gse, 4),
                                               bcast(r_m1, NT, 4), ALU.is_equal))
    P.emit("vector", lambda v: v.scalar_tensor_tensor(A(r_gs), A(r_gp), -BIG, A(r_gse),
                                                      ALU.mult, ALU.add))
    P.emit("vector", lambda v: v.reduce_max(A(r_m2), v3(r_gs, 4), AX.X))
    P.emit("vector", lambda v: v.tensor_tensor(v3(r_gp, 4), v3(r_gse, 4),
                                               bcast(r_m2, NT, 4), ALU.is_ge))
    P.emit("vector", lambda v: v.tensor_scalar(
        bass.AP(r_t1, 0, [[NT * 16, 128], [4, NT * 4], [1, 4]]),
        bass.AP(r_gp, 0, [[NT * 4, 128], [1, NT * 4], [0, 4]]), 1.0, None, ALU.mult))
    P.emit("vector", lambda v: v.tensor_scalar(A(r_t2), A(r_sv), 1.0, None, ALU.add))
    P.emit("vector", lambda v: v.tensor_tensor(A(r_msk), A(r_t2), A(r_t1), ALU.mult))
    P.emit("vector", lambda v: v.tensor_scalar(A(r_msk), A(r_msk), 1.0, None,
                                               ALU.subtract))
    P.emit("vector", lambda v: v.scalar_tensor_tensor(A(r_t1), A(eio_f), -TIE, A(r_msk),
                                                      ALU.mult, ALU.add))
    P.emit("vector", lambda v: v.reduce_max(A(r_m1), v3(r_t1, 16), AX.X))
    P.emit("vector", lambda v: v.tensor_tensor(v3(r_t2, 16), v3(r_t1, 16),
                                               bcast(r_m1, NT, 16), ALU.is_equal))
    P.emit("vector", lambda v: v.scalar_tensor_tensor(A(r_msk), A(r_t2), -BIG, A(r_t1),
                                                      ALU.mult, ALU.add))
    for it in range(2):
        mb = r_m2 if it == 0 else r_m1
        P.emit("vector", lambda v, mb=mb: v.reduce_max(A(mb), v3(r_msk, 16), AX.X))
        P.emit("vector", lambda v, mb=mb: v.tensor_tensor(
            v3(r_t2, 16), v3(r_msk, 16), bcast(mb, NT, 16), ALU.is_equal))
        P.emit("vector", lambda v: v.scalar_tensor_tensor(
            A(r_msk), A(r_t2), -BIG, A(r_msk), ALU.mult, ALU.add))
    P.emit("vector", lambda v: v.reduce_max(A(r_m2), v3(r_msk, 16), AX.X))
    P.emit("vector", lambda v: v.tensor_tensor(v3(r_t2, 16), v3(r_t1, 16),
                                               bcast(r_m2, NT, 16), ALU.is_ge))
    P.emit("vector", lambda v: v.tensor_tensor(A(r_msk), A(r_sig), A(r_t2), ALU.mult))
    P.emit("vector", lambda v: v.reduce_sum(A(r_m1), v3(r_msk, 16), AX.X))
    P.emit("vector", lambda v: v.tensor_scalar(A(r_m1), A(r_m1), 1e-20, None, ALU.add))
    P.emit("vector", lambda v: v.reciprocal(A(r_m2), A(r_m1)))
    P.emit("vector", lambda v: v.tensor_scalar(A(r_t1), A(r_msk), 2.5, None, ALU.mult))
    tok_c = P.emit("vector", lambda v: v.tensor_tensor(
        v3(c_sb, 16), v3(r_t1, 16), bcast(r_m2, NT, 16), ALU.mult), inc="VE")
    pevac[2] = tok_lg

    if NT > 4:
        emit_aT(FT_ORDER[8:], 8)
    else:
        emit_aT(FT_ORDER, 0)

    tok_b_last = None
    shuc_cons = [None, None]
    tok_shuc = list(shuc_pre)
    npi = 0
    for ft in range(16):
        slw = ft % 2
        if ft >= 2:
            tok_shuc[slw] = cast_dma(f"SHUC{slw}", A(shuc[slw]),
                bass.AP(shu_h, ft * 128, [[2 * NF, 128], [2 * NF * 128, ND], [1, 128]]),
                waits=[shuc_cons[slw]])
        for th in range(max(1, NTH)):
            pi = npi % 2
            npi += 1
            ps = PS[pi]
            tg = None
            ng = min(2, (T - th * 1024) // 512)
            for g in range(ng):
                for dc in range(ND):
                    w = ([tok_shuc[slw], pevac[pi]] if (g == 0 and dc == 0) else ())
                    tg = P.emit("tensor", lambda t, ft=ft, th=th, g=g, dc=dc, ps=ps,
                                slw=slw: t.matmul(
                        A(ps)[:, g * 512:(g + 1) * 512],
                        A(shuc[slw])[:, dc * 128:(dc + 1) * 128],
                        A(xnT_pf)[:, dc * T + th * 1024 + g * 512:
                                  dc * T + th * 1024 + (g + 1) * 512],
                        start=(dc == 0), stop=(dc == ND - 1)), waits=w,
                        inc=("TE" if (g == ng - 1 and dc == ND - 1) else None))
            shuc_cons[slw] = tg
            tok_b_last = P.emit("vector", lambda v, ps=ps, ft=ft, th=th, ng=ng:
                                v.tensor_tensor(
                A(za)[:, ft * T + th * 1024: ft * T + th * 1024 + ng * 512],
                A(ps)[:, 0:ng * 512],
                A(za)[:, ft * T + th * 1024: ft * T + th * 1024 + ng * 512],
                ALU.mult), waits=[tg, tok_a_last], inc="VE")
            pevac[pi] = tok_b_last

    tok_accinit = [None] * NT
    tok_shdn = None
    for tt in range(NT):
        pi = 2 + tt % 2
        ps = PS[pi]
        tg = None
        for ds in range(2):
            for fc in range(16):
                w = ([tok_shd, tok_b_last, pevac[pi]] if (ds == 0 and fc == 0) else ())
                tg = P.emit("tensor", lambda t, tt=tt, ds=ds, fc=fc, ps=ps: t.matmul(
                    A(ps)[:, ds * 512:(ds + 1) * 512],
                    A(za)[:, fc * T + tt * 128: fc * T + (tt + 1) * 128],
                    A(shd_sb)[:, fc * D + ds * 512: fc * D + (ds + 1) * 512],
                    start=(fc == 0), stop=(fc == 15)), waits=w,
                    inc=("TE" if (ds == 1 and fc == 15) else None))
        tok_shdn = tg
        tok_accinit[tt] = P.emit("vector", lambda v, ps=ps, tt=tt: v.tensor_copy(
            A(acc)[:, tt * D:(tt + 1) * D], A(ps)),
            waits=[tg, tok_all("SS0"), tok_all("SS1")], inc="VE")
        pevac[pi] = tok_accinit[tt]

    ge_cons = [None, None]
    ue_cons = None
    de_cons = None
    tok_y = {tt: tok_accinit[tt] for tt in range(NT)}
    tok_zlast = None
    for e in range(NE):
        sl = e % 2
        tok_ge = cast_dma(f"GE{sl}", A(ge[sl]), bass.AP(gw_h, e * D * NF,
                                             [[NF, 128], [NF * 128, ND], [1, NF]]),
                          waits=[ge_cons[sl], (tok_shdn if e == 1 else None)]
                          + (([tok_gau_last] + tok_xnpf) if e == 0 else []))
        tok_ue = cast_dma("UE", A(ue_t), bass.AP(uw_h, e * D * NF,
                                             [[NF, 128], [NF * 128, ND], [1, NF]]),
                          waits=[ue_cons, (tok_shdn if e == 0 else None),
                                 (tok_c if e == 0 else None)])
        tok_de = cast_dma("DE", A(de), bass.AP(dw_h, e * NF * D,
                                         [[D, 128], [D * 128, ND], [1, D]]),
                          waits=[de_cons, (tok_shdn if e == 0 else None)])
        npi = 0
        for ft in range(ND):
            for th in range(max(1, NTH)):
                pi = npi % 2
                pi2 = 2 + npi % 2
                npi += 1
                ps = PS[pi]
                ps2 = PS[pi2]
                tg = None
                ng = min(2, (T - th * 1024) // 512)
                for g in range(ng):
                    for dc in range(ND):
                        w = []
                        if g == 0 and dc == 0:
                            w = [tok_ge, pevac[pi]]
                            if e == 0:
                                w.append(tok_b_last)
                        tg = P.emit("tensor", lambda t, ft=ft, th=th, g=g, dc=dc,
                                    ps=ps, sl=sl: t.matmul(
                            A(ps)[:, g * 512:(g + 1) * 512],
                            A(ge[sl])[:, dc * NF + ft * 128: dc * NF + (ft + 1) * 128],
                            A(xnT_pf)[:, dc * T + th * 1024 + g * 512:
                                      dc * T + th * 1024 + (g + 1) * 512],
                            start=(dc == 0), stop=(dc == ND - 1)), waits=w,
                            inc=("TE" if (g == ng - 1 and dc == ND - 1) else None))
                ge_cons[sl] = tg
                tok_h = P.emit("scalar", lambda s, ps=ps, ft=ft, th=th, ng=ng:
                               s.activation(
                    A(z_sb)[:, ft * T + th * 1024: ft * T + th * 1024 + ng * 512],
                    A(ps)[:, 0:ng * 512], AF.Silu), waits=[tg], inc="SE")
                pevac[pi] = tok_h
                tg = None
                for g in range(ng):
                    for dc in range(ND):
                        w = ([tok_ue, pevac[pi2]] if (g == 0 and dc == 0) else ())
                        tg = P.emit("tensor", lambda t, ft=ft, th=th, g=g, dc=dc,
                                    ps=ps2, sl=sl: t.matmul(
                            A(ps)[:, g * 512:(g + 1) * 512],
                            A(ue_t)[:, dc * NF + ft * 128: dc * NF + (ft + 1) * 128],
                            A(xnT_pf)[:, dc * T + th * 1024 + g * 512:
                                      dc * T + th * 1024 + (g + 1) * 512],
                            start=(dc == 0), stop=(dc == ND - 1)), waits=w,
                            inc=("TE" if (g == ng - 1 and dc == ND - 1) else None))
                ue_cons = tg
                tok_u = P.emit("vector", lambda v, ps=ps2, ft=ft, th=th, ng=ng:
                               v.tensor_tensor(
                    A(z_sb)[:, ft * T + th * 1024: ft * T + th * 1024 + ng * 512],
                    A(ps)[:, 0:ng * 512],
                    A(z_sb)[:, ft * T + th * 1024: ft * T + th * 1024 + ng * 512],
                    ALU.mult), waits=[tg, tok_h], inc="VE")
                pevac[pi2] = tok_u
                tok_zlast = tok_u
        for tt in range(NT):
            pi = tt % 2
            ps = PS[pi]
            tg = None
            for ds in range(2):
                for fc in range(ND):
                    w = ([tok_de, tok_zlast, pevac[pi]] if (ds == 0 and fc == 0) else ())
                    tg = P.emit("tensor", lambda t, tt=tt, ds=ds, fc=fc, ps=ps: t.matmul(
                        A(ps)[:, ds * 512:(ds + 1) * 512],
                        A(z_sb)[:, fc * T + tt * 128: fc * T + (tt + 1) * 128],
                        A(de)[:, fc * D + ds * 512: fc * D + (ds + 1) * 512],
                        start=(fc == 0), stop=(fc == ND - 1)), waits=w,
                        inc=("TE" if (ds == 1 and fc == ND - 1) else None))
            tok_y[tt] = P.emit("vector", lambda v, ps=ps, tt=tt, e=e:
                               v.scalar_tensor_tensor(
                A(acc)[:, tt * D:(tt + 1) * D], A(ps),
                A(c_sb)[:, tt * 16 + e: tt * 16 + e + 1],
                A(acc)[:, tt * D:(tt + 1) * D], ALU.mult, ALU.add),
                waits=[tg, tok_c, tok_accinit[tt]], inc="VE")
            pevac[pi] = tok_y[tt]
        de_cons = tg

    tok_fin = None
    tok_mf = None
    sin_cons = [None, None]
    xin_cons = [None, None]
    obuf_cons = [None, None]
    wdead = [ge_cons[0], ge_cons[1], ue_cons, de_cons]
    for tt in range(NT):
        sl = tt % 2
        tok_si = sdma(f"SI{sl}", A(sin[sl]), A(sp_s)[tt * 128:(tt + 1) * 128, :],
                      waits=[tok_all("SS0"), tok_all("SS1"), sin_cons[sl]]
                      + (wdead if tt == 0 else []))
        tok_xi = sdma(f"XI{sl}", A(xin2[sl]), A(x_h)[tt * 128:(tt + 1) * 128, :],
                      waits=[xin_cons[sl]] + (wdead if tt == 0 else []))
        tok_gf = P.emit("scalar", lambda s, tt=tt: s.activation(
            A(gfin), A(acc)[:, tt * D:(tt + 1) * D], AF.Silu),
            waits=[tok_y[tt], tok_mf], inc="SE")
        tok_mf = P.emit("vector", lambda v, sl=sl: v.tensor_tensor(
            A(mfin), A(sin[sl]), A(gfin), ALU.mult),
            waits=[tok_gf, tok_si], inc="VE")
        tok_ov = P.emit("vector", lambda v, sl=sl: v.tensor_tensor(
            A(obuf[sl]), A(mfin), A(xin2[sl]), ALU.add),
            waits=[tok_xi, obuf_cons[sl]], inc="VE")
        sin_cons[sl] = tok_ov
        xin_cons[sl] = tok_ov
        tok_fin = sdma(f"OB{sl}", A(out_h)[tt * 128:(tt + 1) * 128, :], A(obuf[sl]),
                       waits=[tok_ov])
        obuf_cons[sl] = tok_fin

    P.emit("sync", lambda s: s.nop(), waits=[tok_all("OB0"), tok_all("OB1")])

    with nc.Block() as block:
        block.tensor(P.replay("tensor"))
        block.vector(P.replay("vector"))
        block.scalar(P.replay("scalar"))
        block.gpsimd(P.replay("gpsimd"))
        block.sync(P.replay("sync"))
    return nc


_NC_CACHE = {}


def get_program(T=2048):
    if T not in _NC_CACHE:
        _NC_CACHE[T] = build_program(T)
    return _NC_CACHE[T]


def kernel(**inputs):
    x = np.asarray(inputs["L0_attn_x"])
    nc = get_program(2048)
    wm = {
        "ln_g": inputs["L0_attn_ln_g"], "ln_b": inputs["L0_attn_ln_b"],
        "Wh": inputs["L0_attn_Wh"], "bh": inputs["L0_attn_bh"],
        "Wqk": inputs["L0_attn_Wqk"], "bqk": inputs["L0_attn_bqk"],
        "gamma": inputs["L0_attn_gamma"], "beta": inputs["L0_attn_beta"],
        "Wo": inputs["L0_attn_Wo"], "bo": inputs["L0_attn_bo"],
        "router_w": inputs["L0_ffn_router_w"], "bias_r": inputs["L0_ffn_bias"],
        "gate_w": inputs["L0_ffn_gate_w"], "up_w": inputs["L0_ffn_up_w"],
        "down_w": inputs["L0_ffn_down_w"],
        "sh_gate": inputs["L0_ffn_sh_gate_w"], "sh_up": inputs["L0_ffn_sh_up_w"],
        "sh_down": inputs["L0_ffn_sh_down_w"],
    }
    wm = {k: np.ascontiguousarray(np.asarray(v), dtype=np.float32) for k, v in wm.items()}
    in_maps = []
    for b in range(8):
        m = dict(wm)
        m["x"] = np.ascontiguousarray(x[b], dtype=np.float32)
        in_maps.append(m)
    res = run_bass_kernel_spmd(nc, in_maps, core_ids=list(range(8)))
    return np.stack([res.results[b]["out"] for b in range(8)], axis=0)


# version 20
# speedup vs baseline: 1.0431x; 1.0431x over previous
import numpy as np
import concourse.bass as bass
import concourse.mybir as mybir
from concourse.bass_utils import run_bass_kernel_spmd

F32 = mybir.dt.float32
BF16 = mybir.dt.bfloat16
I32 = mybir.dt.int32
AF = mybir.ActivationFunctionType
ALU = mybir.AluOpType
AX = mybir.AxisListType

D = 1024
HID = 2048
QK = 128
NE = 16
NF = 1024
ND = 8
EPS = 1e-5
TIE = 1e-6
BIG = 1e9
KB = 1024


class Prog:
    ENGINES = ("tensor", "vector", "scalar", "gpsimd", "sync")

    def __init__(self, nc):
        self.nc = nc
        self.stream = {e: [] for e in self.ENGINES}
        self.sems = {}
        self.count = {}
        self.waited = {}

    def sem(self, name):
        if name not in self.sems:
            self.sems[name] = self.nc.alloc_semaphore(name)
            self.count[name] = 0
        return self.sems[name]

    CHAIN = {"vector": "VE", "scalar": "SE", "gpsimd": "GE"}

    def emit(self, eng, fn, waits=(), inc=None, inc_n=1, chain=True):
        ws = []
        if chain and eng in self.CHAIN:
            own = self.CHAIN[eng]
            assert inc is None or inc == own
            inc = own
            inc_n = 1
            self.sem(own)
            if self.count[own] > 0:
                ws.append((own, self.count[own]))
        for t in waits:
            if t is None:
                continue
            s, v = t
            key = (eng, s)
            if self.waited.get(key, 0) >= v:
                continue
            self.waited[key] = v
            ws.append((s, v))
        tok = None
        if inc is not None:
            self.sem(inc)
            self.count[inc] += inc_n
            tok = (inc, self.count[inc])
        self.stream[eng].append((ws, fn, tok, inc_n))
        return tok

    def replay(self, eng):
        def body(obj):
            for ws, fn, tok, inc_n in self.stream[eng]:
                for (s, v) in ws:
                    obj.wait_ge(self.sems[s], v)
                inst = fn(obj)
                if tok is not None:
                    inst.then_inc(self.sems[tok[0]], inc_n)
        return body


def build_program(T):
    NT = T // 128
    NSL = T // 512
    NTH = T // 1024
    RN = 1.0 / T
    nc = bass.Bass("TRN2", target_bir_lowering=False)

    def din(name, shape):
        return nc.dram_tensor(name, shape, F32, kind="ExternalInput")

    x_h = din("x", [T, D])
    lng_h = din("ln_g", [D]); lnb_h = din("ln_b", [D])
    wh_h = din("Wh", [D, 2 * HID]); bh_h = din("bh", [2 * HID])
    wqk_h = din("Wqk", [D, QK]); bqk_h = din("bqk", [QK])
    gam_h = din("gamma", [2, QK]); bet_h = din("beta", [2, QK])
    wo_h = din("Wo", [HID, D]); bo_h = din("bo", [D])
    rw_h = din("router_w", [NE, D]); rb_h = din("bias_r", [NE])
    gw_h = din("gate_w", [NE, D, NF]); uw_h = din("up_w", [NE, D, NF])
    dw_h = din("down_w", [NE, NF, D])
    shg_h = din("sh_gate", [D, 2 * NF]); shu_h = din("sh_up", [D, 2 * NF])
    shd_h = din("sh_down", [2 * NF, D])
    out_h = nc.dram_tensor("out", [T, D], F32, kind="ExternalOutput")

    sp_xn = nc.dram_tensor("sp_xn", [T, D], BF16, kind="Internal")
    sp_lo = nc.dram_tensor("sp_lo", [T, D], BF16, kind="Internal")
    sp_nm = nc.dram_tensor("sp_nm", [T, D], BF16, kind="Internal")
    sp_gate = nc.dram_tensor("sp_gate", [HID, T], BF16, kind="Internal")
    sp_s = nc.dram_tensor("sp_s", [T, D], BF16, kind="Internal")

    def sb(name, shape, dtype, off):
        assert off >= 16512 and off % 32 == 0, (name, off)
        return nc.alloc_sbuf_tensor_at(name, shape, dtype, offset=off)

    B0 = 16512
    RA = B0 + 16896
    RB = RA + 32 * KB
    RC = RB + 32 * KB
    RD = RC + 64 * KB
    assert RD + 60 * KB <= 229344

    ones_bf = sb("ones_bf", [1, 128], BF16, B0 + 0)
    ones_f = sb("ones_f", [1, 128], F32, B0 + 256)
    svecT = sb("svecT", [128, 8], F32, B0 + 768)
    svecN = sb("svecN", [128, 4], F32, B0 + 800)
    bhg = sb("bhg", [128, 16], F32, B0 + 832)
    epsc = sb("epsc", [128, 1], F32, B0 + 896)
    nhalf = sb("nhalf", [128, 1], F32, B0 + 928)
    phalf = sb("phalf", [128, 1], F32, B0 + 960)
    bias16 = sb("bias16", [128, 16], F32, B0 + 992)
    c_sb = sb("c_sb", [128, NT * 16], F32, B0 + 1056)
    masks = sb("masks", [128, 4 * 512], BF16, B0 + 2080)
    qT = sb("qT", [128, T], BF16, B0 + 2080 + 4 * KB)
    kT = sb("kT", [128, T], BF16, B0 + 2080 + 8 * KB)
    bo_row = sb("bo_row", [1, D], BF16, B0 + 2080 + 12 * KB)
    ident_bf = sb("ident_bf", [128, 128], BF16, B0 + 16 * KB + 32)


    wo_sb = sb("wo_sb", [128, 16 * D], BF16, RA)
    xnT_pf = sb("xnT_pf", [128, ND * T], BF16, RA)

    normedT = sb("normedT", [128, ND * T], BF16, RB)
    relu_t = sb("relu_t", [128, 1024], F32, RB)
    xnin = [sb(f"xnin{i}", [128, D], BF16, RB + 4 * KB + i * 2 * KB) for i in range(8)]
    sout = [sb(f"sout{i}", [128, D], BF16, RB + 20 * KB + i * 2 * KB) for i in range(2)]
    acc = sb("acc", [128, NT * D], BF16, RB)

    xnT_lo = sb("xnT_lo", [128, ND * T], BF16, RC)
    _xwo = [RC + 32 * KB, RC + 36 * KB, RA, RA + 4 * KB]
    xw = [sb(f"xw{i}", [128, D], F32, _xwo[i]) for i in range(4)]
    _xno = [RC + 40 * KB, RC + 44 * KB, RA + 8 * KB, RA + 12 * KB]
    xnf = [sb(f"xnf{i}", [128, D], F32, _xno[i]) for i in range(4)]
    sqj = sb("sqj", [128, D], F32, RC + 48 * KB)
    _hio = [RC + 52 * KB, RC + 54 * KB, RA + 16 * KB, RA + 18 * KB]
    hi_t = [sb(f"hi_t{i}", [128, D], BF16, _hio[i]) for i in range(4)]
    _loo = [RC + 56 * KB, RC + 58 * KB, RA + 20 * KB, RA + 22 * KB]
    lo_t = [sb(f"lo_t{i}", [128, D], BF16, _loo[i]) for i in range(4)]
    _nmo = [RC + 60 * KB, RC + 62 * KB, RA + 24 * KB, RA + 26 * KB]
    nm_t = [sb(f"nm_t{i}", [128, D], BF16, _nmo[i]) for i in range(4)]
    v_sb = sb("v_sb", [128, NT * HID], BF16, RC)
    za = sb("za", [128, 16 * T], BF16, RC)
    z_sb = sb("z_sb", [128, ND * T], BF16, RC)
    de = sb("de", [128, ND * D], BF16, RC + 32 * KB)
    sin = [sb(f"sin{i}", [128, D], BF16, RC + 48 * KB + i * 2 * KB) for i in range(2)]
    xin2 = [sb(f"xin2{i}", [128, D], F32, RC + 52 * KB + i * 4 * KB) for i in range(2)]
    gfin = sb("gfin", [128, D], BF16, RC + 60 * KB)

    rstage = sb("rstage", [16, D], F32, RD + 16 * KB)
    identf = sb("identf", [128, 128], F32, RD + 20 * KB)
    svec_stage = sb("svec_stage", [8, 128], F32, RD + 20 * KB + 512)
    bhg_stage = sb("bhg_stage", [16, 128], F32, RD + 21 * KB)
    lgb_stage = sb("lgb_stage", [16, 128], F32, RD + 21 * KB + 512)
    lgb = sb("lgb", [128, 16], F32, B0 + 16 * KB + 288)
    rT_f32 = sb("rT_f32", [128, 128], F32, RD + 60 * KB)
    rT_hi = sb("rT_hi", [128, 128], BF16, RD + 60 * KB + 512)
    rT_lo = sb("rT_lo", [128, 128], BF16, RD + 60 * KB + 768)
    brow16 = sb("brow16", [1, 16], F32, RD + 61 * KB)
    iota_i = sb("iota_i", [128, 512], I32, RD + 32 * KB)
    iota_f = sb("iota_f", [128, 512], F32, RD + 34 * KB)
    eio_i = sb("eio_i", [128, NT * 16], I32, RD + 36 * KB)
    eio_f = sb("eio_f", [128, NT * 16], F32, RD + 62 * KB)
    gio_i = sb("gio_i", [128, NT * 4], I32, RD + 38 * KB)
    gio_f = sb("gio_f", [128, NT * 4], F32, RD + 63 * KB)
    stb = RD + 38 * KB + 512
    st_ssx = sb("st_ssx", [128, NT], F32, stb)
    st_ssq = sb("st_ssq", [128, NT], F32, stb + 64)
    st_m = sb("st_m", [128, NT], F32, stb + 128)
    st_msq = sb("st_msq", [128, NT], F32, stb + 192)
    st_var = sb("st_var", [128, NT], F32, stb + 256)
    st_l = sb("st_l", [128, NT], F32, stb + 320)
    st_r1 = sb("st_r1", [128, NT], F32, stb + 384)
    st_sq1 = sb("st_sq1", [128, NT], F32, stb + 448)
    st_v2 = sb("st_v2", [128, NT], F32, stb + 512)
    st_Sg = sb("st_Sg", [128, NT], F32, stb + 576)
    st_rho = sb("st_rho", [128, NT], F32, stb + 640)
    logit_sb = sb("logit_sb", [128, NT * 16], F32, RB)
    r_sig = sb("r_sig", [128, NT * 16], F32, RB + 1 * KB)
    r_sv = sb("r_sv", [128, NT * 16], F32, RB + 2 * KB)
    r_t1 = sb("r_t1", [128, NT * 16], F32, RB + 3 * KB)
    r_t2 = sb("r_t2", [128, NT * 16], F32, RB + 4 * KB)
    r_msk = sb("r_msk", [128, NT * 16], F32, RB + 5 * KB)
    r_gs = sb("r_gs", [128, NT * 4], F32, RB + 6 * KB)
    r_gse = sb("r_gse", [128, NT * 4], F32, RB + 6 * KB + 256)
    r_gp = sb("r_gp", [128, NT * 4], F32, RB + 6 * KB + 512)
    r_m1 = sb("r_m1", [128, NT], F32, RB + 6 * KB + 768)
    r_m2 = sb("r_m2", [128, NT], F32, RB + 6 * KB + 832)
    wv = [sb(f"wv{i}", [128, ND * 1024], BF16, RD + i * 16 * KB) for i in range(2)]
    attnT = sb("attnT", [128, 16 * 512], BF16, RD)
    gbuf = [sb(f"gbuf{i}", [128, 16 * 512], BF16, RD + 16 * KB + i * 16 * KB)
            for i in range(2)]
    whg_a = sb("whg_a", [128, ND * 1024], BF16, RD)
    whg_b = sb("whg_b", [128, ND * 1024], BF16, RD + 16 * KB)
    gout = [sb(f"gout{i}", [128, 1024], BF16, RD + 52 * KB + i * 2 * KB)
            for i in range(2)]
    qk_sil = sb("qk_sil", [128, 512], F32, RD + 58 * KB)
    bh_row = sb("bh_row", [1, HID], BF16, RD + 48 * KB)
    wqk_sb = sb("wqk_sb", [128, ND * 128], BF16, RD + 56 * KB)
    shd_sb = sb("shd_sb", [128, 16 * D], BF16, RD + 16 * KB)
    ge = [sb(f"ge{i}", [128, ND * NF], BF16, RD + i * 16 * KB) for i in range(2)]
    ue_t = sb("ue_t", [128, ND * NF], BF16, RD + 32 * KB)
    shgc = [sb(f"shgc{i}", [128, ND * 128], BF16, RD + 48 * KB + i * 2 * KB)
            for i in range(2)]
    shuc = [sb(f"shuc{i}", [128, ND * 128], BF16, RD + 52 * KB + i * 2 * KB)
            for i in range(2)]
    mfin = sb("mfin", [128, D], F32, RD + 56 * KB)
    obuf = [sb(f"obuf{i}", [128, D], F32, RD + 32 * KB + i * 4 * KB)
            for i in range(2)]

    PS = [nc.alloc_psum_tensor(f"PS{i}", [128, 1024], F32) for i in range(4)]

    P = Prog(nc)
    for s in ("TE", "VE", "SE", "GE"):
        P.sem(s)

    A = lambda t: t.ap()

    def cast_dma(sem, dst_ap, src_ap, waits=()):
        return P.emit("gpsimd", lambda g, d=dst_ap, s=src_ap: g.dma_start(d, s),
                      waits=waits, inc=sem, inc_n=16, chain=False)

    def sdma(sem, dst_ap, src_ap, waits=()):
        return P.emit("sync", lambda s, d=dst_ap, so=src_ap: s.dma_start(d, so),
                      waits=waits, inc=sem, inc_n=16)

    def sdma_tp(sem, dst_ap, src_ap, waits=()):
        return P.emit("sync", lambda s, d=dst_ap, so=src_ap: s.dma_start(
            d, so, transpose=True), waits=waits, inc=sem, inc_n=16)

    def adma(sem, dst_ap, src_ap, waits=()):
        return P.emit("scalar", lambda s, d=dst_ap, so=src_ap: s.dma_start(d, so),
                      waits=waits, inc=sem, inc_n=16, chain=False)

    def tok_all(sem):
        return (sem, P.count.get(sem, 0))

    tok_wqk = cast_dma("WQK", A(wqk_sb), bass.AP(wqk_h, 0, [[QK, 128], [QK * 128, ND], [1, QK]]))
    tok_wv = [None, None]
    tok_wv[0] = cast_dma("WV0", A(wv[0]), bass.AP(wh_h, 0,
        [[2 * HID, 128], [2 * HID * 128, ND], [1, 1024]]))
    tok_bh = cast_dma("BH", A(bh_row), bass.AP(bh_h, 0, [[HID, 1], [1, HID]]))
    tok_bo = cast_dma("BO", A(bo_row), bass.AP(bo_h, 0, [[D, 1], [1, D]]))

    tok_ones = P.emit("vector", lambda v: v.memset(A(ones_bf), 1.0), inc="VE")
    P.emit("vector", lambda v: v.memset(A(ones_f), 1.0))
    P.emit("vector", lambda v: v.memset(A(epsc), EPS))
    P.emit("vector", lambda v: v.memset(A(nhalf), -0.5))
    tok_consts = P.emit("vector", lambda v: v.memset(A(phalf), 0.5), inc="VE")

    tok_io = P.emit("gpsimd", lambda g: g.iota(
        A(iota_i), [[1, 512]], channel_multiplier=-1,
        allow_small_or_imprecise_dtypes=True), inc="GE")
    tok_eio = P.emit("gpsimd", lambda g: g.iota(
        A(eio_i), [[0, NT], [1, 16]], channel_multiplier=0,
        allow_small_or_imprecise_dtypes=True), inc="GE")
    tok_gio = P.emit("gpsimd", lambda g: g.iota(
        A(gio_i), [[0, NT], [1, 4]], channel_multiplier=0,
        allow_small_or_imprecise_dtypes=True), inc="GE")
    P.emit("vector", lambda v: v.tensor_copy(A(iota_f), A(iota_i)), waits=[tok_io])
    tok_identf = P.emit("vector", lambda v: v.tensor_scalar(
        A(identf), A(iota_f)[:, 0:128], 0.0, None, ALU.is_equal), inc="VE")
    tok_identb = P.emit("vector", lambda v: v.tensor_scalar(
        A(ident_bf), A(iota_f)[:, 0:128], 0.0, None, ALU.is_equal), inc="VE")
    tok_masks = None
    for oi, o in enumerate((0, 128, 256, 384)):
        tok_masks = P.emit("vector", lambda v, oi=oi, o=o: v.tensor_scalar(
            A(masks)[:, oi * 512:(oi + 1) * 512], A(iota_f), float(o), None, ALU.is_ge),
            inc="VE")
    P.emit("vector", lambda v: v.tensor_copy(A(eio_f), A(eio_i)), waits=[tok_eio])
    tok_iof = P.emit("vector", lambda v: v.tensor_copy(A(gio_f), A(gio_i)),
                     waits=[tok_gio], inc="VE")

    tok_lng = sdma("LGBA", A(lgb_stage)[0:8, :], bass.AP(lng_h, 0, [[128, 8], [1, 128]]))
    tok_lnb = sdma("CONST", A(lgb_stage)[8:16, :], bass.AP(lnb_h, 0, [[128, 8], [1, 128]]),
                   waits=[tok_lng])
    tok_rst = sdma("CONST", A(rstage)[0:16, :], A(rw_h))
    tok_brow = sdma("CONST", A(brow16), bass.AP(rb_h, 0, [[NE, 1], [1, NE]]))
    tok_svs = None
    for row, src in ((0, bass.AP(bqk_h, 0, [[QK, 1], [1, QK]])),
                     (1, A(gam_h)[0:1, :]), (2, A(bet_h)[0:1, :]),
                     (3, A(gam_h)[1:2, :]), (4, A(bet_h)[1:2, :])):
        tok_svs = sdma("CONST", A(svec_stage)[row:row + 1, :], src)
    tok_bhgs = sdma("CONST", A(bhg_stage)[0:16, :], bass.AP(bh_h, HID, [[128, 16], [1, 128]]))

    tok_const = tok_all("CONST")
    tok_lng = tok_lnb = tok_rst = tok_brow = tok_svs = tok_bhgs = tok_const

    tok_bi16m = P.emit("tensor", lambda t: t.matmul(
        A(PS[1])[:, 0:16], A(ones_f), A(brow16), start=True, stop=True),
        waits=[tok_brow, tok_consts], inc="TE")
    tok_bi16 = P.emit("vector", lambda v: v.tensor_copy(A(bias16), A(PS[1])[:, 0:16]),
                      waits=[tok_bi16m], inc="VE")
    tok_lgbT = P.emit("tensor", lambda t: t.transpose(
        A(PS[1])[:, 48:64], A(lgb_stage)[0:16, :], A(identf)[0:16, 0:16]),
        waits=[tok_lng, tok_lnb, tok_identf, tok_bi16], inc="TE")
    tok_lgb = P.emit("vector", lambda v: v.tensor_copy(A(lgb), A(PS[1])[:, 48:64]),
                     waits=[tok_lgbT], inc="VE")
    tok_svT = P.emit("tensor", lambda t: t.transpose(
        A(PS[1])[:, 16:24], A(svec_stage), A(identf)[0:8, 0:8]),
        waits=[tok_svs, tok_identf, tok_bi16], inc="TE")
    tok_svec = P.emit("vector", lambda v: v.tensor_copy(A(svecT), A(PS[1])[:, 16:24]),
                      waits=[tok_svT], inc="VE")
    tok_svn = None
    for i in range(4):
        tok_svn = P.emit("vector", lambda v, i=i: v.tensor_scalar(
            A(svecN)[:, i:i + 1], A(svecT)[:, i + 1:i + 2], RN, None, ALU.mult),
            inc="VE")
    tok_bhT = P.emit("tensor", lambda t: t.transpose(
        A(PS[1])[:, 32:48], A(bhg_stage)[0:16, :], A(identf)[0:16, 0:16]),
        waits=[tok_bhgs, tok_svec], inc="TE")
    tok_bhg = P.emit("vector", lambda v: v.tensor_copy(A(bhg), A(PS[1])[:, 32:48]),
                     waits=[tok_bhT], inc="VE")

    tok_sq = [None] * NT
    tok_sx = [None] * NT
    tok_hi = [None] * NT
    tok_lo = [None] * NT
    tok_nm = [None] * NT
    tok_sphi = [None] * NT
    tok_splo = [None] * NT
    tok_spnm = [None] * NT
    for k in range(NT):
        sl = k % 4
        w = [tok_sq[k - 4], tok_sx[k - 4]] if k >= 4 else []
        tok_x = sdma(f"XA{sl}", A(xw[sl]), A(x_h)[k * 128:(k + 1) * 128, :], waits=w)
        tok_sq[k] = P.emit("scalar", lambda s, sl=sl, k=k: s.activation(
            A(sqj), A(xw[sl]), AF.Square, accum_out=A(st_ssq)[:, k:k + 1]),
            waits=[tok_x], inc="SE")
        tok_sx[k] = P.emit("vector", lambda v, sl=sl, k=k: v.reduce_sum(
            A(st_ssx)[:, k:k + 1], A(xw[sl]), AX.X), waits=[tok_x], inc="VE")
    P.emit("vector", lambda v: v.tensor_scalar(A(st_m), A(st_ssx), 1.0 / D, None,
                                               ALU.mult), waits=[tok_sx[NT - 1]])
    P.emit("vector", lambda v: v.tensor_scalar(A(st_msq), A(st_ssq), 1.0 / D, None,
                                               ALU.mult), waits=[tok_sq[NT - 1]])
    P.emit("vector", lambda v: v.tensor_tensor(A(st_var), A(st_m), A(st_m), ALU.mult))
    tok_var = P.emit("vector", lambda v: v.tensor_tensor(A(st_var), A(st_msq),
                                                         A(st_var), ALU.subtract),
                     inc="VE")
    P.emit("scalar", lambda s: s.activation(A(st_l), A(st_var), AF.Ln, bias=A(epsc)),
           waits=[tok_var])
    P.emit("scalar", lambda s: s.activation(A(st_r1), A(st_l), AF.Exp, scale=A(nhalf)))
    tok_r1 = P.emit("scalar", lambda s: s.activation(A(st_sq1), A(st_l), AF.Exp,
                                                     scale=A(phalf)), inc="SE")
    P.emit("vector", lambda v: v.tensor_tensor(A(st_v2), A(st_var), A(st_r1), ALU.mult),
           waits=[tok_r1])
    tok_v2 = P.emit("vector", lambda v: v.tensor_tensor(A(st_v2), A(st_v2), A(st_r1),
                                                        ALU.mult), inc="VE")
    P.emit("scalar", lambda s: s.activation(A(st_l), A(st_v2), AF.Ln, bias=A(epsc)),
           waits=[tok_v2])
    tok_Sg = P.emit("scalar", lambda s: s.activation(A(st_Sg), A(st_l), AF.Exp,
                                                     scale=A(nhalf)), inc="SE")
    tok_rho = P.emit("vector", lambda v: v.tensor_tensor(A(st_rho), A(st_Sg),
                                                         A(st_sq1), ALU.mult),
                     waits=[tok_Sg], inc="VE")
    for k in range(NT):
        sl = k % 4
        w = [tok_rho]
        if k < 4:
            w += [tok_sq[NT - 4 + k], tok_sx[NT - 4 + k]]
        else:
            w += [tok_nm[k - 4], tok_hi[k - 4], tok_sphi[k - 4], tok_splo[k - 4],
                  tok_spnm[k - 4]]
        tok_x = sdma(f"XA{sl}", A(xw[sl]), A(x_h)[k * 128:(k + 1) * 128, :], waits=w)
        tok_xn = P.emit("vector", lambda v, sl=sl, k=k: v.tensor_scalar(
            A(xnf[sl]), A(xw[sl]), A(st_m)[:, k:k + 1], A(st_r1)[:, k:k + 1],
            ALU.subtract, ALU.mult), waits=[tok_x], inc="VE")
        tok_nm[k] = P.emit("vector", lambda v, sl=sl, k=k: v.tensor_scalar(
            A(nm_t[sl]), A(xnf[sl]), A(st_rho)[:, k:k + 1], None, ALU.mult),
            inc="VE")
        tok_hi[k] = P.emit("scalar", lambda s, sl=sl: s.copy(A(hi_t[sl]), A(xnf[sl])),
                           waits=[tok_xn], inc="SE")
        tok_lo[k] = P.emit("gpsimd", lambda g, sl=sl: g.tensor_tensor(
            A(lo_t[sl]), A(xnf[sl]), A(hi_t[sl]), ALU.subtract),
            waits=[tok_hi[k]], inc="GE")
        hh = k // max(1, NT // 2)
        tok_spnm[k] = adma(f"SPN{sl}{hh}", A(sp_nm)[k * 128:(k + 1) * 128, :],
                           A(nm_t[sl]), waits=[tok_nm[k]])
        tok_sphi[k] = adma(f"SPH{sl}{hh}", A(sp_xn)[k * 128:(k + 1) * 128, :],
                           A(hi_t[sl]), waits=[tok_hi[k]])
        tok_splo[k] = adma(f"SPL{sl}{hh}", A(sp_lo)[k * 128:(k + 1) * 128, :],
                           A(lo_t[sl]), waits=[tok_lo[k]])
    sph_h0 = [tok_all(f"SPH{s}0") for s in range(4)]
    spl_h0 = [tok_all(f"SPL{s}0") for s in range(4)]
    spn_h0 = [tok_all(f"SPN{s}0") for s in range(4)]


    sph_all = sph_h0 + [tok_all(f"SPH{s}1") for s in range(4)]
    spl_all = spl_h0 + [tok_all(f"SPL{s}1") for s in range(4)]
    spn_all = spn_h0 + [tok_all(f"SPN{s}1") for s in range(4)]
    tok_wo = cast_dma("WO", A(wo_sb),
                      bass.AP(wo_h, 0, [[D, 128], [D * 128, 16], [1, D]]),
                      waits=sph_all + spl_all + spn_all)
    TH2 = T // 2
    for dc in range(ND):
        sdma_tp("TPN0", A(normedT)[:, dc * T: dc * T + TH2],
                A(sp_nm)[0:TH2, dc * 128:(dc + 1) * 128], waits=spn_h0)
    tok_tpn0 = tok_all("TPN0")
    tok_tp_nm_h0 = None
    for dc in range(ND):
        tok_tp_nm_h0 = P.emit("vector", lambda v, dc=dc: v.tensor_scalar(
            A(normedT)[:, dc * T: dc * T + TH2],
            A(normedT)[:, dc * T: dc * T + TH2],
            A(lgb)[:, dc:dc + 1], A(lgb)[:, 8 + dc: 9 + dc], ALU.mult, ALU.add),
            waits=[tok_tpn0, tok_lgb], inc="VE")
    for dc in range(ND):
        sdma_tp("TPN1", A(normedT)[:, dc * T + TH2: (dc + 1) * T],
                A(sp_nm)[TH2:T, dc * 128:(dc + 1) * 128], waits=spn_all)
    tok_tpn1 = tok_all("TPN1")
    tok_tp_nm_h1 = None
    for dc in range(ND):
        tok_tp_nm_h1 = P.emit("vector", lambda v, dc=dc: v.tensor_scalar(
            A(normedT)[:, dc * T + TH2: (dc + 1) * T],
            A(normedT)[:, dc * T + TH2: (dc + 1) * T],
            A(lgb)[:, dc:dc + 1], A(lgb)[:, 8 + dc: 9 + dc], ALU.mult, ALU.add),
            waits=[tok_tpn1], inc="VE")
    tok_tp_nm = [tok_tp_nm_h0, tok_tp_nm_h1]

    tok_rTt = None
    for dc in range(ND):
        tok_rTt = P.emit("tensor", lambda t, dc=dc: t.transpose(
            A(PS[1])[:, 64 + dc * 16: 64 + (dc + 1) * 16],
            A(rstage)[0:16, dc * 128:(dc + 1) * 128], A(identf)[0:16, 0:16]),
            waits=[tok_rst, tok_identf, tok_bhg], inc=("TE" if dc == ND - 1 else None))
    tok_wv[1] = cast_dma("WV1", A(wv[1]), bass.AP(wh_h, 1024,
        [[2 * HID, 128], [2 * HID * 128, ND], [1, 1024]]), waits=[tok_rTt])
    tok_rTe = P.emit("vector", lambda v: v.tensor_copy(A(rT_f32), A(PS[1])[:, 64:192]),
                     waits=[tok_rTt], inc="VE")
    P.emit("vector", lambda v: v.tensor_copy(A(rT_hi), A(rT_f32)))
    tok_rT = P.emit("vector", lambda v: v.tensor_tensor(A(rT_lo), A(rT_f32), A(rT_hi),
                                                        ALU.subtract), inc="VE")

    tok_logits = None

    def emit_logits(tts, w0):
        nonlocal tok_logits
        for tt in tts:
            mms = []
            for dc in range(ND):
                mms.append((A(xnT_pf)[:, dc * T + tt * 128: dc * T + (tt + 1) * 128],
                            A(rT_hi)[:, dc * 16:(dc + 1) * 16]))
            for dc in range(ND):
                mms.append((A(xnT_pf)[:, dc * T + tt * 128: dc * T + (tt + 1) * 128],
                            A(rT_lo)[:, dc * 16:(dc + 1) * 16]))
            for dc in range(ND):
                mms.append((A(xnT_lo)[:, dc * T + tt * 128: dc * T + (tt + 1) * 128],
                            A(rT_hi)[:, dc * 16:(dc + 1) * 16]))
            n = len(mms)
            for i, (l, r) in enumerate(mms):
                w = w0 if (tt == tts[0] and i == 0) else ()
                tok_logits = P.emit("tensor", lambda t, l=l, r=r, st=(i == 0),
                                    sp=(i == n - 1), tt=tt: t.matmul(
                    A(PS[2])[:, tt * 16:(tt + 1) * 16], l, r, start=st, stop=sp),
                    waits=w, inc=("TE" if i == n - 1 else None))

    pevac = {0: None, 1: tok_rTe, 2: None, 3: None}
    tok_qk_done = [None] * NSL

    def emit_qk(slcs, w0):
        for slc in slcs:
            ps = PS[slc % 2]
            w = [tok_wqk, pevac[slc % 2]] + list(w0)
            tg = None
            for dc in range(ND):
                tg = P.emit("tensor", lambda t, dc=dc, slc=slc, ps=ps: t.matmul(
                    A(ps)[:, 0:512], A(wqk_sb)[:, dc * 128:(dc + 1) * 128],
                    A(normedT)[:, dc * T + slc * 512: dc * T + (slc + 1) * 512],
                    start=(dc == 0), stop=(dc == ND - 1)),
                    waits=(w if dc == 0 else ()), inc=("TE" if dc == ND - 1 else None))
            tok_se = P.emit("scalar", lambda s, ps=ps: s.activation(
                A(qk_sil), A(ps)[:, 0:512], AF.Silu, bias=A(svecT)[:, 0:1]),
                waits=[tg, tok_svn, tok_svec,
                       (tok_qk_done[slc - 1] if slc >= 1 else None)], inc="SE")
            pevac[slc % 2] = tok_se
            P.emit("vector", lambda v, slc=slc: v.tensor_scalar(
                A(qT)[:, slc * 512:(slc + 1) * 512], A(qk_sil),
                A(svecN)[:, 0:1], A(svecN)[:, 1:2], ALU.mult, ALU.add), waits=[tok_se])
            tok_qk_done[slc] = P.emit("vector", lambda v, slc=slc: v.tensor_scalar(
                A(kT)[:, slc * 512:(slc + 1) * 512], A(qk_sil),
                A(svecN)[:, 2:3], A(svecN)[:, 3:4], ALU.mult, ALU.add), inc="VE")

    emit_qk(list(range(max(1, NSL // 2))),
            ([tok_tp_nm_h0] if NSL >= 2 else tok_tp_nm))
    emit_qk(list(range(max(1, NSL // 2), NSL)), tok_tp_nm)

    tok_qkT = tok_qk_done[NSL - 1]

    tok_v = None
    tok_hvv0 = None
    tok_hvv1 = None
    for hs in range(2):
        for tt in range(NT):
            pi = tt % 2
            ps = PS[pi]
            w = [tok_wv[hs], tok_bh, tok_ones, pevac[pi]] + tok_tp_nm
            if tt == min(8, NT - 1):
                w = w + sph_all + spl_all + spn_all
            tg = None
            for hh in range(2):
                for dc in range(ND):
                    tg = P.emit("tensor", lambda t, dc=dc, tt=tt, hh=hh, hs=hs, ps=ps:
                                t.matmul(
                        A(ps)[:, hh * 512:(hh + 1) * 512],
                        A(normedT)[:, dc * T + tt * 128: dc * T + (tt + 1) * 128],
                        A(wv[hs])[:, dc * 1024 + hh * 512: dc * 1024 + (hh + 1) * 512],
                        start=(dc == 0), stop=False),
                        waits=(w if (hh == 0 and dc == 0) else ()))
                tg = P.emit("tensor", lambda t, hh=hh, hs=hs, ps=ps: t.matmul(
                    A(ps)[:, hh * 512:(hh + 1) * 512], A(ones_bf),
                    A(bh_row)[:, hs * 1024 + hh * 512: hs * 1024 + (hh + 1) * 512],
                    start=False, stop=True), inc=("TE" if hh == 1 else None))
            if hs == 0 and tt == NT - 1:
                tok_hvv0 = tg
            if hs == 1 and tt == NT - 1:
                tok_hvv1 = tg
            tok_v = P.emit("scalar", lambda s, ps=ps, tt=tt, hs=hs: s.activation(
                A(v_sb)[:, tt * HID + hs * 1024: tt * HID + (hs + 1) * 1024],
                A(ps), AF.Silu), waits=[tg], inc="SE")
            pevac[pi] = tok_v

    tok_whga = cast_dma("WHGA", A(whg_a), bass.AP(wh_h, HID,
        [[2 * HID, 128], [2 * HID * 128, ND], [1, 1024]]), waits=[tok_hvv0])
    tok_whgb = cast_dma("WHGB", A(whg_b), bass.AP(wh_h, HID + 1024,
        [[2 * HID, 128], [2 * HID * 128, ND], [1, 1024]]), waits=[tok_hvv1])
    gout_cons = [None, None]
    tok_gsp = None
    gs_half = None
    gidx = 0
    for slh in range(max(1, NTH)):
        for ht in range(16):
            wbuf = whg_a if ht < 8 else whg_b
            wtok = tok_whga if ht < 8 else tok_whgb
            pi = gidx % 2
            ps = PS[pi]
            gsl = gidx % 2
            gidx += 1
            w = [wtok, pevac[pi]] + tok_tp_nm
            tg = None
            ng = min(2, (T - slh * 1024) // 512)
            for g in range(ng):
                for dc in range(ND):
                    tg = P.emit("tensor", lambda t, dc=dc, slh=slh, g=g, ps=ps,
                                wbuf=wbuf, ht=ht: t.matmul(
                        A(ps)[:, g * 512:(g + 1) * 512],
                        A(wbuf)[:, dc * 1024 + (ht % 8) * 128:
                                dc * 1024 + (ht % 8) * 128 + 128],
                        A(normedT)[:, dc * T + slh * 1024 + g * 512:
                                   dc * T + slh * 1024 + (g + 1) * 512],
                        start=(dc == 0), stop=(dc == ND - 1)),
                        waits=(w if (g == 0 and dc == 0) else ()),
                        inc=("TE" if (g == ng - 1 and dc == ND - 1) else None))
            go = gout[gsl]
            tok_gev = P.emit("scalar", lambda s, ps=ps, go=go, ht=ht, ng=ng: s.activation(
                A(go)[:, 0:ng * 512], A(ps)[:, 0:ng * 512], AF.Silu,
                bias=A(bhg)[:, ht:ht + 1]),
                waits=[tg, tok_bhg, gout_cons[gsl]], inc="SE")
            pevac[pi] = tok_gev
            tok_gsp = sdma(f"GS{gsl}{slh}", A(sp_gate)[ht * 128:(ht + 1) * 128,
                                      slh * 1024: slh * 1024 + ng * 512],
                           A(go)[:, 0:ng * 512], waits=[tok_gev])
            gout_cons[gsl] = tok_gsp
        if slh == 0:
            gs_half = [tok_all("GS00"), tok_all("GS10")]

    gs_all = gs_half + [tok_all("GS01"), tok_all("GS11")]
    tok_gin = [None, None]
    gin_cons = [None, None]
    strip_cons = None
    tok_s_spill = [None] * NT
    xnin_cons = [None] * 8
    sout_cons = [None, None]
    tok_gau_last = None
    tok_xnin = [None] * 8
    pending = None

    def emit_gau(p):
        nonlocal tok_gau_last, pending
        isl, slg, tok_gt = p
        for q in range(4):
            tt = isl * 4 + q
            xs = tt % 8
            ps = PS[3]
            tg = None
            for ds in range(2):
                for ht in range(16):
                    w = ([tok_gt, tok_wo, tok_bo, pevac[3]]
                         if (ds == 0 and ht == 0) else ())
                    tg = P.emit("tensor", lambda t, ht=ht, ds=ds, q=q, slg=slg, ps=ps:
                                t.matmul(
                        A(ps)[:, ds * 512:(ds + 1) * 512],
                        A(gbuf[slg])[:, ht * 512 + q * 128: ht * 512 + (q + 1) * 128],
                        A(wo_sb)[:, ht * 1024 + ds * 512: ht * 1024 + (ds + 1) * 512],
                        start=(ht == 0), stop=False), waits=w)
                tg = P.emit("tensor", lambda t, ds=ds, ps=ps: t.matmul(
                    A(ps)[:, ds * 512:(ds + 1) * 512], A(ones_bf),
                    A(bo_row)[:, ds * 512:(ds + 1) * 512], start=False, stop=True),
                    inc=("TE" if ds == 1 else None))
            tok_gau_last = tg
            so = sout[tt % 2]
            tok_sev = P.emit("vector", lambda v, ps=ps, so=so, xs=xs: v.tensor_tensor(
                A(so), A(ps), A(xnin[xs]), ALU.add),
                waits=[tg, tok_xnin[xs], sout_cons[tt % 2]], inc="VE")
            pevac[3] = tok_sev
            xnin_cons[xs] = tok_sev
            tok_s_spill[tt] = adma(f"SS{tt % 2}", A(sp_s)[tt * 128:(tt + 1) * 128, :],
                                   A(so), waits=[tok_sev])
            sout_cons[tt % 2] = tok_s_spill[tt]
        gin_cons[slg] = tok_gau_last

    for isl in range(NSL):
        i0 = isl * 512
        jmax = 4 * (isl + 1)
        slg = isl % 2
        tok_gin[slg] = sdma(f"GIN{slg}", A(gbuf[slg]),
                            bass.AP(sp_gate, i0, [[T, 128], [128 * T, 16], [1, 512]]),
                            waits=(gs_half if isl < 2 else gs_all)
                            + [gin_cons[slg]])
        for q in range(4):
            tt = isl * 4 + q
            xs = tt % 8
            tok_xnin[xs] = sdma(f"XN{xs}", A(xnin[xs]),
                                A(sp_xn)[tt * 128:(tt + 1) * 128, :],
                                waits=sph_all + [xnin_cons[xs]])
        tok_strip = None
        for jp in range(0, jmax, 2):
            pi = (jp // 2) % 2
            ps = PS[pi]
            npair = min(2, jmax - jp)
            tg = None
            for u in range(npair):
                jt = jp + u
                w = [tok_qkT, pevac[pi]]
                tg = P.emit("tensor", lambda t, jt=jt, u=u, ps=ps, isl=isl: t.matmul(
                    A(ps)[:, u * 512:(u + 1) * 512],
                    A(kT)[:, jt * 128:(jt + 1) * 128],
                    A(qT)[:, isl * 512:(isl + 1) * 512],
                    start=True, stop=True),
                    waits=(w if u == 0 else ()), inc=("TE" if u == npair - 1 else None))
            w = [tg, tok_masks, strip_cons] + (gs_all if (isl == 0 and jp == 0) else [])
            P.emit("vector", lambda v, ps=ps, npair=npair: v.tensor_scalar(
                A(relu_t)[:, 0:npair * 512], A(ps)[:, 0:npair * 512], 0.0, None,
                ALU.max), waits=w)
            tok_strip = P.emit("vector", lambda v, jp=jp, npair=npair: v.tensor_tensor(
                A(attnT)[:, jp * 512:(jp + npair) * 512], A(relu_t)[:, 0:npair * 512],
                A(relu_t)[:, 0:npair * 512], ALU.mult), inc="VE")
            pevac[pi] = tok_strip
            for u in range(npair):
                jt = jp + u
                oi = jt - 4 * isl
                if oi >= 0:
                    tok_strip = P.emit("vector", lambda v, jt=jt, oi=oi: v.tensor_tensor(
                        A(attnT)[:, jt * 512:(jt + 1) * 512],
                        A(attnT)[:, jt * 512:(jt + 1) * 512],
                        A(masks)[:, oi * 512:(oi + 1) * 512], ALU.mult), inc="VE")
                    pevac[pi] = tok_strip
        tok_gt_last = None
        for hp in range(0, 16, 2):
            ps = PS[2]
            tg = None
            for hh in range(2):
                ht = hp + hh
                for jt in range(jmax):
                    w = [tok_strip, tok_v, pevac[2]] if (jt == 0 and hh == 0) else ()
                    tg = P.emit("tensor", lambda t, jt=jt, ht=ht, hh=hh, ps=ps,
                                jmax=jmax: t.matmul(
                        A(ps)[:, hh * 512:(hh + 1) * 512],
                        A(v_sb)[:, jt * HID + ht * 128: jt * HID + (ht + 1) * 128],
                        A(attnT)[:, jt * 512:(jt + 1) * 512],
                        start=(jt == 0), stop=(jt == jmax - 1)),
                        waits=w, inc=("TE" if (hh == 1 and jt == jmax - 1) else None))
            strip_cons = tg
            tok_gt_last = P.emit("vector", lambda v, hp=hp, ps=ps, slg=slg:
                                 v.tensor_tensor(
                A(gbuf[slg])[:, hp * 512:(hp + 2) * 512], A(ps),
                A(gbuf[slg])[:, hp * 512:(hp + 2) * 512], ALU.mult),
                waits=[tg, tok_gin[slg]], inc="VE")
            pevac[2] = tok_gt_last
        if pending is not None:
            emit_gau(pending)
        pending = (isl, slg, tok_gt_last)
    emit_gau(pending)

    for dc in range(ND):
        sdma_tp("TPF", A(xnT_pf)[:, dc * T:(dc + 1) * T],
                A(sp_xn)[:, dc * 128:(dc + 1) * 128],
                waits=sph_all + [tok_gau_last])
    tok_xnpf = [tok_all("TPF")]
    for dc in range(ND):
        sdma_tp("TPL0", A(xnT_lo)[:, dc * T: dc * T + TH2],
                A(sp_lo)[0:TH2, dc * 128:(dc + 1) * 128],
                waits=spl_h0 + [strip_cons])
    for dc in range(ND):
        sdma_tp("TPL1", A(xnT_lo)[:, dc * T + TH2: (dc + 1) * T],
                A(sp_lo)[TH2:T, dc * 128:(dc + 1) * 128], waits=spl_all)
    tok_tp_lo = [tok_all("TPL0"), tok_all("TPL1")]

    pevac = {0: None, 1: None, 2: None, 3: None}
    tok_a_last = None
    shgc_cons = [None, None]
    tok_shgc = [None, None]
    shuc_pre = [None, None]
    FT_ORDER = list(range(8, 16)) + list(range(8)) if NT > 4 else list(range(16))
    for i2 in range(2):
        ft = FT_ORDER[i2]
        tok_shgc[i2] = cast_dma(f"SHGC{i2}", A(shgc[i2]),
            bass.AP(shg_h, ft * 128, [[2 * NF, 128], [2 * NF * 128, ND], [1, 128]]),
            waits=[tok_gau_last])
    for i2 in range(2):
        shuc_pre[i2] = cast_dma(f"SHUC{i2}", A(shuc[i2]),
            bass.AP(shu_h, i2 * 128, [[2 * NF, 128], [2 * NF * 128, ND], [1, 128]]),
            waits=[tok_gau_last] + tok_xnpf)
    tok_shd = cast_dma("SHD", A(shd_sb),
                       bass.AP(shd_h, 0, [[D, 128], [D * 128, 16], [1, D]]),
                       waits=[tok_gau_last] + tok_xnpf)
    npi = 0

    def emit_aT(fts, base):
        nonlocal tok_a_last, npi
        for idx, ft in enumerate(fts):
            slw = (base + idx) % 2
            if base + idx >= 2:
                tok_shgc[slw] = cast_dma(f"SHGC{slw}", A(shgc[slw]),
                    bass.AP(shg_h, ft * 128,
                            [[2 * NF, 128], [2 * NF * 128, ND], [1, 128]]),
                    waits=[shgc_cons[slw]])
            for th in range(max(1, NTH)):
                pi = npi % 2
                npi += 1
                ps = PS[pi]
                tg = None
                ng = min(2, (T - th * 1024) // 512)
                for g in range(ng):
                    for dc in range(ND):
                        w = (([tok_shgc[slw], pevac[pi]] + tok_xnpf)
                             if (g == 0 and dc == 0) else ())
                        tg = P.emit("tensor", lambda t, ft=ft, th=th, g=g, dc=dc, ps=ps,
                                    slw=slw: t.matmul(
                            A(ps)[:, g * 512:(g + 1) * 512],
                            A(shgc[slw])[:, dc * 128:(dc + 1) * 128],
                            A(xnT_pf)[:, dc * T + th * 1024 + g * 512:
                                      dc * T + th * 1024 + (g + 1) * 512],
                            start=(dc == 0), stop=(dc == ND - 1)), waits=w,
                            inc=("TE" if (g == ng - 1 and dc == ND - 1) else None))
                shgc_cons[slw] = tg
                tok_a_last = P.emit("scalar", lambda s, ps=ps, ft=ft, th=th, ng=ng:
                                    s.activation(
                    A(za)[:, ft * T + th * 1024: ft * T + th * 1024 + ng * 512],
                    A(ps)[:, 0:ng * 512], AF.Silu), waits=[tg], inc="SE")
                pevac[pi] = tok_a_last

    if NT > 4:
        emit_aT(FT_ORDER[:8], 0)
    emit_logits(list(range(NT)), tok_xnpf + tok_tp_lo + [tok_rT])
    tok_lg = P.emit("vector", lambda v: v.tensor_copy(A(logit_sb), A(PS[2])[:, 0:NT * 16]),
                    waits=[tok_logits], inc="VE")
    tok_sig = P.emit("scalar", lambda s: s.activation(A(r_sig), A(logit_sb), AF.Sigmoid),
                     waits=[tok_lg], inc="SE")

    def v3(t, inner):
        return A(t).rearrange("p (g c) -> p g c", c=inner)

    def bcast(t, ng, rep):
        return bass.AP(t, 0, [[ng, 128], [1, ng], [0, rep]])

    P.emit("vector", lambda v: v.tensor_tensor(
        v3(r_sv, 16), v3(r_sig, 16),
        bass.AP(bias16, 0, [[16, 128], [0, NT], [1, 16]]), ALU.add),
        waits=[tok_sig, tok_bi16, tok_iof])

    def cview(i):
        return bass.AP(r_sv, i, [[NT * 16, 128], [4, NT * 4]])

    P.emit("vector", lambda v: v.tensor_tensor(A(r_gs), cview(0), cview(1), ALU.add))
    for (a, b) in ((0, 2), (0, 3), (1, 2), (1, 3), (2, 3)):
        P.emit("vector", lambda v, a=a, b=b: v.tensor_tensor(A(r_gp), cview(a), cview(b),
                                                             ALU.add))
        P.emit("vector", lambda v: v.tensor_tensor(A(r_gs), A(r_gs), A(r_gp), ALU.max))
    P.emit("vector", lambda v: v.scalar_tensor_tensor(A(r_gse), A(gio_f), -TIE, A(r_gs),
                                                      ALU.mult, ALU.add))
    P.emit("vector", lambda v: v.reduce_max(A(r_m1), v3(r_gse, 4), AX.X))
    P.emit("vector", lambda v: v.tensor_tensor(v3(r_gp, 4), v3(r_gse, 4),
                                               bcast(r_m1, NT, 4), ALU.is_equal))
    P.emit("vector", lambda v: v.scalar_tensor_tensor(A(r_gs), A(r_gp), -BIG, A(r_gse),
                                                      ALU.mult, ALU.add))
    P.emit("vector", lambda v: v.reduce_max(A(r_m2), v3(r_gs, 4), AX.X))
    P.emit("vector", lambda v: v.tensor_tensor(v3(r_gp, 4), v3(r_gse, 4),
                                               bcast(r_m2, NT, 4), ALU.is_ge))
    P.emit("vector", lambda v: v.tensor_scalar(
        bass.AP(r_t1, 0, [[NT * 16, 128], [4, NT * 4], [1, 4]]),
        bass.AP(r_gp, 0, [[NT * 4, 128], [1, NT * 4], [0, 4]]), 1.0, None, ALU.mult))
    P.emit("vector", lambda v: v.tensor_scalar(A(r_t2), A(r_sv), 1.0, None, ALU.add))
    P.emit("vector", lambda v: v.tensor_tensor(A(r_msk), A(r_t2), A(r_t1), ALU.mult))
    P.emit("vector", lambda v: v.tensor_scalar(A(r_msk), A(r_msk), 1.0, None,
                                               ALU.subtract))
    P.emit("vector", lambda v: v.scalar_tensor_tensor(A(r_t1), A(eio_f), -TIE, A(r_msk),
                                                      ALU.mult, ALU.add))
    P.emit("vector", lambda v: v.reduce_max(A(r_m1), v3(r_t1, 16), AX.X))
    P.emit("vector", lambda v: v.tensor_tensor(v3(r_t2, 16), v3(r_t1, 16),
                                               bcast(r_m1, NT, 16), ALU.is_equal))
    P.emit("vector", lambda v: v.scalar_tensor_tensor(A(r_msk), A(r_t2), -BIG, A(r_t1),
                                                      ALU.mult, ALU.add))
    for it in range(2):
        mb = r_m2 if it == 0 else r_m1
        P.emit("vector", lambda v, mb=mb: v.reduce_max(A(mb), v3(r_msk, 16), AX.X))
        P.emit("vector", lambda v, mb=mb: v.tensor_tensor(
            v3(r_t2, 16), v3(r_msk, 16), bcast(mb, NT, 16), ALU.is_equal))
        P.emit("vector", lambda v: v.scalar_tensor_tensor(
            A(r_msk), A(r_t2), -BIG, A(r_msk), ALU.mult, ALU.add))
    P.emit("vector", lambda v: v.reduce_max(A(r_m2), v3(r_msk, 16), AX.X))
    P.emit("vector", lambda v: v.tensor_tensor(v3(r_t2, 16), v3(r_t1, 16),
                                               bcast(r_m2, NT, 16), ALU.is_ge))
    P.emit("vector", lambda v: v.tensor_tensor(A(r_msk), A(r_sig), A(r_t2), ALU.mult))
    P.emit("vector", lambda v: v.reduce_sum(A(r_m1), v3(r_msk, 16), AX.X))
    P.emit("vector", lambda v: v.tensor_scalar(A(r_m1), A(r_m1), 1e-20, None, ALU.add))
    P.emit("vector", lambda v: v.reciprocal(A(r_m2), A(r_m1)))
    P.emit("vector", lambda v: v.tensor_scalar(A(r_t1), A(r_msk), 2.5, None, ALU.mult))
    tok_c = P.emit("vector", lambda v: v.tensor_tensor(
        v3(c_sb, 16), v3(r_t1, 16), bcast(r_m2, NT, 16), ALU.mult), inc="VE")
    pevac[2] = tok_lg

    if NT > 4:
        emit_aT(FT_ORDER[8:], 8)
    else:
        emit_aT(FT_ORDER, 0)

    tok_b_last = None
    shuc_cons = [None, None]
    tok_shuc = list(shuc_pre)
    npi = 0
    for ft in range(16):
        slw = ft % 2
        if ft >= 2:
            tok_shuc[slw] = cast_dma(f"SHUC{slw}", A(shuc[slw]),
                bass.AP(shu_h, ft * 128, [[2 * NF, 128], [2 * NF * 128, ND], [1, 128]]),
                waits=[shuc_cons[slw]])
        for th in range(max(1, NTH)):
            pi = npi % 2
            npi += 1
            ps = PS[pi]
            tg = None
            ng = min(2, (T - th * 1024) // 512)
            for g in range(ng):
                for dc in range(ND):
                    w = ([tok_shuc[slw], pevac[pi]] if (g == 0 and dc == 0) else ())
                    tg = P.emit("tensor", lambda t, ft=ft, th=th, g=g, dc=dc, ps=ps,
                                slw=slw: t.matmul(
                        A(ps)[:, g * 512:(g + 1) * 512],
                        A(shuc[slw])[:, dc * 128:(dc + 1) * 128],
                        A(xnT_pf)[:, dc * T + th * 1024 + g * 512:
                                  dc * T + th * 1024 + (g + 1) * 512],
                        start=(dc == 0), stop=(dc == ND - 1)), waits=w,
                        inc=("TE" if (g == ng - 1 and dc == ND - 1) else None))
            shuc_cons[slw] = tg
            tok_b_last = P.emit("vector", lambda v, ps=ps, ft=ft, th=th, ng=ng:
                                v.tensor_tensor(
                A(za)[:, ft * T + th * 1024: ft * T + th * 1024 + ng * 512],
                A(ps)[:, 0:ng * 512],
                A(za)[:, ft * T + th * 1024: ft * T + th * 1024 + ng * 512],
                ALU.mult), waits=[tg, tok_a_last], inc="VE")
            pevac[pi] = tok_b_last

    tok_accinit = [None] * NT
    tok_shdn = None
    for tt in range(NT):
        pi = 2 + tt % 2
        ps = PS[pi]
        tg = None
        for ds in range(2):
            for fc in range(16):
                w = ([tok_shd, tok_b_last, pevac[pi]] if (ds == 0 and fc == 0) else ())
                tg = P.emit("tensor", lambda t, tt=tt, ds=ds, fc=fc, ps=ps: t.matmul(
                    A(ps)[:, ds * 512:(ds + 1) * 512],
                    A(za)[:, fc * T + tt * 128: fc * T + (tt + 1) * 128],
                    A(shd_sb)[:, fc * D + ds * 512: fc * D + (ds + 1) * 512],
                    start=(fc == 0), stop=(fc == 15)), waits=w,
                    inc=("TE" if (ds == 1 and fc == 15) else None))
        tok_shdn = tg
        tok_accinit[tt] = P.emit("vector", lambda v, ps=ps, tt=tt: v.tensor_copy(
            A(acc)[:, tt * D:(tt + 1) * D], A(ps)),
            waits=[tg, tok_all("SS0"), tok_all("SS1")], inc="VE")
        pevac[pi] = tok_accinit[tt]

    ge_cons = [None, None]
    ue_cons = None
    de_cons = None
    tok_y = {tt: tok_accinit[tt] for tt in range(NT)}
    tok_zlast = None
    for e in range(NE):
        sl = e % 2
        tok_ge = cast_dma(f"GE{sl}", A(ge[sl]), bass.AP(gw_h, e * D * NF,
                                             [[NF, 128], [NF * 128, ND], [1, NF]]),
                          waits=[ge_cons[sl], (tok_shdn if e == 1 else None)]
                          + (([tok_gau_last] + tok_xnpf) if e == 0 else []))
        tok_ue = cast_dma("UE", A(ue_t), bass.AP(uw_h, e * D * NF,
                                             [[NF, 128], [NF * 128, ND], [1, NF]]),
                          waits=[ue_cons, (tok_shdn if e == 0 else None),
                                 (tok_c if e == 0 else None)])
        tok_de = cast_dma("DE", A(de), bass.AP(dw_h, e * NF * D,
                                         [[D, 128], [D * 128, ND], [1, D]]),
                          waits=[de_cons, (tok_shdn if e == 0 else None)])
        npi = 0
        for ft in range(ND):
            for th in range(max(1, NTH)):
                pi = npi % 2
                pi2 = 2 + npi % 2
                npi += 1
                ps = PS[pi]
                ps2 = PS[pi2]
                tg = None
                ng = min(2, (T - th * 1024) // 512)
                for g in range(ng):
                    for dc in range(ND):
                        w = []
                        if g == 0 and dc == 0:
                            w = [tok_ge, pevac[pi]]
                            if e == 0:
                                w.append(tok_b_last)
                        tg = P.emit("tensor", lambda t, ft=ft, th=th, g=g, dc=dc,
                                    ps=ps, sl=sl: t.matmul(
                            A(ps)[:, g * 512:(g + 1) * 512],
                            A(ge[sl])[:, dc * NF + ft * 128: dc * NF + (ft + 1) * 128],
                            A(xnT_pf)[:, dc * T + th * 1024 + g * 512:
                                      dc * T + th * 1024 + (g + 1) * 512],
                            start=(dc == 0), stop=(dc == ND - 1)), waits=w,
                            inc=("TE" if (g == ng - 1 and dc == ND - 1) else None))
                ge_cons[sl] = tg
                tok_h = P.emit("scalar", lambda s, ps=ps, ft=ft, th=th, ng=ng:
                               s.activation(
                    A(z_sb)[:, ft * T + th * 1024: ft * T + th * 1024 + ng * 512],
                    A(ps)[:, 0:ng * 512], AF.Silu), waits=[tg], inc="SE")
                pevac[pi] = tok_h
                tg = None
                for g in range(ng):
                    for dc in range(ND):
                        w = ([tok_ue, pevac[pi2]] if (g == 0 and dc == 0) else ())
                        tg = P.emit("tensor", lambda t, ft=ft, th=th, g=g, dc=dc,
                                    ps=ps2, sl=sl: t.matmul(
                            A(ps)[:, g * 512:(g + 1) * 512],
                            A(ue_t)[:, dc * NF + ft * 128: dc * NF + (ft + 1) * 128],
                            A(xnT_pf)[:, dc * T + th * 1024 + g * 512:
                                      dc * T + th * 1024 + (g + 1) * 512],
                            start=(dc == 0), stop=(dc == ND - 1)), waits=w,
                            inc=("TE" if (g == ng - 1 and dc == ND - 1) else None))
                ue_cons = tg
                tok_u = P.emit("vector", lambda v, ps=ps2, ft=ft, th=th, ng=ng:
                               v.tensor_tensor(
                    A(z_sb)[:, ft * T + th * 1024: ft * T + th * 1024 + ng * 512],
                    A(ps)[:, 0:ng * 512],
                    A(z_sb)[:, ft * T + th * 1024: ft * T + th * 1024 + ng * 512],
                    ALU.mult), waits=[tg, tok_h], inc="VE")
                pevac[pi2] = tok_u
                tok_zlast = tok_u
        for tt in range(NT):
            pi = tt % 2
            ps = PS[pi]
            tg = None
            for ds in range(2):
                for fc in range(ND):
                    w = ([tok_de, tok_zlast, pevac[pi]] if (ds == 0 and fc == 0) else ())
                    tg = P.emit("tensor", lambda t, tt=tt, ds=ds, fc=fc, ps=ps: t.matmul(
                        A(ps)[:, ds * 512:(ds + 1) * 512],
                        A(z_sb)[:, fc * T + tt * 128: fc * T + (tt + 1) * 128],
                        A(de)[:, fc * D + ds * 512: fc * D + (ds + 1) * 512],
                        start=(fc == 0), stop=(fc == ND - 1)), waits=w,
                        inc=("TE" if (ds == 1 and fc == ND - 1) else None))
            tok_y[tt] = P.emit("vector", lambda v, ps=ps, tt=tt, e=e:
                               v.scalar_tensor_tensor(
                A(acc)[:, tt * D:(tt + 1) * D], A(ps),
                A(c_sb)[:, tt * 16 + e: tt * 16 + e + 1],
                A(acc)[:, tt * D:(tt + 1) * D], ALU.mult, ALU.add),
                waits=[tg, tok_c, tok_accinit[tt]], inc="VE")
            pevac[pi] = tok_y[tt]
        de_cons = tg

    tok_fin = None
    tok_mf = None
    sin_cons = [None, None]
    xin_cons = [None, None]
    obuf_cons = [None, None]
    wdead = [ge_cons[0], ge_cons[1], ue_cons, de_cons]
    for tt in range(NT):
        sl = tt % 2
        tok_si = sdma(f"SI{sl}", A(sin[sl]), A(sp_s)[tt * 128:(tt + 1) * 128, :],
                      waits=[tok_all("SS0"), tok_all("SS1"), sin_cons[sl]]
                      + (wdead if tt == 0 else []))
        tok_xi = sdma(f"XI{sl}", A(xin2[sl]), A(x_h)[tt * 128:(tt + 1) * 128, :],
                      waits=[xin_cons[sl]] + (wdead if tt == 0 else []))
        tok_gf = P.emit("scalar", lambda s, tt=tt: s.activation(
            A(gfin), A(acc)[:, tt * D:(tt + 1) * D], AF.Silu),
            waits=[tok_y[tt], tok_mf], inc="SE")
        tok_mf = P.emit("vector", lambda v, sl=sl: v.tensor_tensor(
            A(mfin), A(sin[sl]), A(gfin), ALU.mult),
            waits=[tok_gf, tok_si], inc="VE")
        tok_ov = P.emit("vector", lambda v, sl=sl: v.tensor_tensor(
            A(obuf[sl]), A(mfin), A(xin2[sl]), ALU.add),
            waits=[tok_xi, obuf_cons[sl]], inc="VE")
        sin_cons[sl] = tok_ov
        xin_cons[sl] = tok_ov
        tok_fin = sdma(f"OB{sl}", A(out_h)[tt * 128:(tt + 1) * 128, :], A(obuf[sl]),
                       waits=[tok_ov])
        obuf_cons[sl] = tok_fin

    P.emit("sync", lambda s: s.nop(), waits=[tok_all("OB0"), tok_all("OB1")])

    with nc.Block() as block:
        block.tensor(P.replay("tensor"))
        block.vector(P.replay("vector"))
        block.scalar(P.replay("scalar"))
        block.gpsimd(P.replay("gpsimd"))
        block.sync(P.replay("sync"))
    return nc


_NC_CACHE = {}


def get_program(T=2048):
    if T not in _NC_CACHE:
        _NC_CACHE[T] = build_program(T)
    return _NC_CACHE[T]


def kernel(**inputs):
    x = np.asarray(inputs["L0_attn_x"])
    nc = get_program(2048)
    wm = {
        "ln_g": inputs["L0_attn_ln_g"], "ln_b": inputs["L0_attn_ln_b"],
        "Wh": inputs["L0_attn_Wh"], "bh": inputs["L0_attn_bh"],
        "Wqk": inputs["L0_attn_Wqk"], "bqk": inputs["L0_attn_bqk"],
        "gamma": inputs["L0_attn_gamma"], "beta": inputs["L0_attn_beta"],
        "Wo": inputs["L0_attn_Wo"], "bo": inputs["L0_attn_bo"],
        "router_w": inputs["L0_ffn_router_w"], "bias_r": inputs["L0_ffn_bias"],
        "gate_w": inputs["L0_ffn_gate_w"], "up_w": inputs["L0_ffn_up_w"],
        "down_w": inputs["L0_ffn_down_w"],
        "sh_gate": inputs["L0_ffn_sh_gate_w"], "sh_up": inputs["L0_ffn_sh_up_w"],
        "sh_down": inputs["L0_ffn_sh_down_w"],
    }
    wm = {k: np.ascontiguousarray(np.asarray(v), dtype=np.float32) for k, v in wm.items()}
    in_maps = []
    for b in range(8):
        m = dict(wm)
        m["x"] = np.ascontiguousarray(x[b], dtype=np.float32)
        in_maps.append(m)
    res = run_bass_kernel_spmd(nc, in_maps, core_ids=list(range(8)))
    return np.stack([res.results[b]["out"] for b in range(8)], axis=0)
